# version 13
# baseline (speedup 1.0000x reference)
import contextlib
import numpy as np
import concourse.bass as bass
import concourse.mybir as mybir
from concourse.bass_utils import run_bass_kernel_spmd

F32 = mybir.dt.float32
BF16 = mybir.dt.bfloat16
F16 = mybir.dt.float16
AO = mybir.AluOpType
AF = mybir.ActivationFunctionType
AX = mybir.AxisListType

MAGIC = 12582912.0
MAGIC65535 = MAGIC + 65535.0
INV7 = 1.0 / 7.0
INV32767 = 1.0 / 32767.0
TINY = 1e-12
EPS_LN = 1e-5
EPS_Z = 1e-8
K216 = 2.0 ** -16

B, L, D, P, I = 8, 2048, 1024, 256, 3072
DC, IC, PC, LC = 8, 24, 2, 16
NSL, SL = 4, 512


class Sem:
    def __init__(self, nc, es, name):
        self.h = es.enter_context(nc.semaphore(name))
        self.n = 0

    def inc(self, instr, k=1):
        instr.then_inc(self.h, k)
        self.n += k
        return instr

    def wait(self, eng, val=None):
        eng.wait_ge(self.h, self.n if val is None else val)


class Env:
    pass


_UNIQ = [0]


def uniq(name):
    _UNIQ[0] += 1
    return f"{name}_u{_UNIQ[0]}"


class Stream:

    def __init__(self, e, name, nslots=2):
        self.sems = [Sem(e.nc, e.es, uniq(f"{name}{i}")) for i in range(nslots)]
        self.nslots = nslots

    def inc(self, instr, j):
        self.sems[j % self.nslots].inc(instr, 16)

    def wait_done(self, eng, j):
        s = self.sems[j % self.nslots]
        eng.wait_ge(s.h, 16 * ((j // self.nslots) + 1))

    def wait_all(self, eng):
        for s in self.sems:
            if s.n:
                eng.wait_ge(s.h, s.n)


class DBuf:
    def __init__(self, nc, es, name, shape, dtype, nslots=2):
        self.t = [es.enter_context(nc.sbuf_tensor(uniq(f"{name}_{i}"), shape, dtype))
                  for i in range(nslots)]
        self.n = nslots

    def __getitem__(self, i):
        return self.t[i % self.n][:]



def build_nc():
    nc = bass.Bass("TRN2", target_bir_lowering=False, debug=False,
                   detect_race_conditions=False)
    e = Env()
    e.nc = nc

    def din(name, shape, dtype=F32):
        return nc.dram_tensor(name, shape, dtype, kind="ExternalInput").ap()

    e.xT = din("xT", [D, L])
    e.wqT = din("wqT", [D, D])
    e.wkT = din("wkT", [D, D])
    e.wvT = din("wvT", [D, D])
    e.proj = din("proj", [D, P])
    e.wup_n = din("wup_n", [I, D])
    e.wgate_n = din("wgate_n", [I, D])
    e.wdown_n = din("wdown_n", [D, I])
    e.ident = din("ident", [128, 128])
    e.onesr = din("onesr", [1, 128])

    e.out = nc.dram_tensor("out", [D, L], F32, kind="ExternalOutput").ap()

    e.xatt_h = nc.dram_tensor("xatt_sp", [DC, 128, L], F32, kind="Internal").ap()
    e.gate_h = nc.dram_tensor("gate_sp", [IC, 128, L], F16, kind="Internal").ap()
    e.up_h = nc.dram_tensor("up_sp", [IC, 128, L], F16, kind="Internal").ap()
    e.gs_h = nc.dram_tensor("gs_sp", [IC, 128, L], F16, kind="Internal").ap()
    e.cc_in = nc.dram_tensor("cc_in", [1, 128], F32, kind="Internal").ap()
    e.cc_out = nc.dram_tensor("cc_out", [1, 256], F32, kind="Internal").ap()

    with contextlib.ExitStack() as es:
        e.es = es
        e.s_ld = Sem(nc, es, "s_ld")
        e.g_ld = Sem(nc, es, "g_ld")
        e.s_st = Sem(nc, es, "s_st")
        e.cc = Sem(nc, es, "cc")
        e.cc_ld = Sem(nc, es, "cc_ld")
        e.store_streams = []
        e.te = Sem(nc, es, "te")
        e.ve = Sem(nc, es, "ve")
        e.se = Sem(nc, es, "se")
        e.ge = Sem(nc, es, "ge")

        e.ident_bf = es.enter_context(nc.sbuf_tensor("ident_bf", [128, 128], BF16))
        e.onesr_s = es.enter_context(nc.sbuf_tensor("onesr_s", [1, 128], F32))
        e.onecf = es.enter_context(nc.sbuf_tensor("onecf", [128, 1], F32))
        e.onecb = es.enter_context(nc.sbuf_tensor("onecb", [128, 1], BF16))
        for nm in ("s1t", "c1t", "s2t", "c2t", "s3t", "c3t", "kkt",
                   "s4t", "c4t", "s5t", "c5t"):
            setattr(e, nm, es.enter_context(nc.sbuf_tensor(nm, [128, 8], F32)))
        e.mloc = es.enter_context(nc.sbuf_tensor("mloc", [128, 1], F32))
        e.magict = es.enter_context(nc.sbuf_tensor("magict", [128, 1], F32))
        e.nmagict = es.enter_context(nc.sbuf_tensor("nmagict", [128, 1], F32))
        e.mxc = es.enter_context(nc.sbuf_tensor("mxc", [128, 32], F32))
        e.vrow = es.enter_context(nc.sbuf_tensor("vrow", [1, 128], F32))
        e.valr = es.enter_context(nc.sbuf_tensor("valr", [1, 8, 8], F32))
        e.ccs = es.enter_context(nc.sbuf_tensor("ccs", [1, 8], F32))
        e.ccr = es.enter_context(nc.sbuf_tensor("ccr", [1, 256], F32))
        e.mfin = es.enter_context(nc.sbuf_tensor("mfin", [128, 8], F32))
        e.mfin2 = es.enter_context(nc.sbuf_tensor("mfin2", [128, 8], F32))
        e.mbt = es.enter_context(nc.sbuf_tensor("mbt", [128, 8], F32))
        e.ps_misc = es.enter_context(nc.psum_tensor("ps_misc", [128, 128], F32))

        e.block = es.enter_context(nc.Block())
        emit_program(e)

    return nc



def wait_stores(e, eng):
    for st in e.store_streams:
        st.wait_all(eng)


def phase_barrier(e):
    for deco in (e.block.sync, e.block.gpsimd, e.block.vector,
                 e.block.scalar, e.block.tensor):
        @deco
        def _(eng):
            for s in (e.te, e.ve, e.se, e.ge, e.s_ld, e.g_ld, e.cc_ld):
                s.wait(eng)
            wait_stores(e, eng)


def stage_maxcols_to_cc(e, mx_ap, ncols, slot):
    @e.block.scalar
    def _(s):
        e.ve.wait(s)
        e.se.inc(s.copy(e.mbt[:], mx_ap[:, 0:8]))

    @e.block.vector
    def _(v):
        e.se.wait(v)
        e.ve.inc(v.tensor_reduce(
            e.mfin[:], mx_ap.rearrange("p (a b) -> p a b", a=8),
            AX.X, AO.max))

    @e.block.scalar
    def _(s):
        e.ve.wait(s)
        e.se.inc(s.copy(e.mfin2[:], e.mfin[:]))

    @e.block.vector
    def _(v):
        e.se.wait(v)
        e.ve.inc(v.tensor_reduce(e.mloc[:], e.mfin2[:], AX.X, AO.max))

    @e.block.scalar
    def _(s):
        e.ve.wait(s)
        e.se.inc(s.copy(e.mbt[:, 0:1], e.mloc[:]))

    @e.block.gpsimd
    def _(g):
        e.se.wait(g)
        e.cc_ld.inc(g.dma_start(
            out=e.cc_in[0:1, 0:128], in_=e.mbt[:, 0:1]), 16)


def collective_max(e, total, slot=0):
    @e.block.gpsimd
    def _(g):
        e.ve.wait(g)
        e.cc_ld.wait(g)
        e.cc.inc(g.collective_compute(
            "AllReduce", AO.max, replica_groups=[list(range(8))],
            ins=[e.cc_in[0:1, 0:128]],
            outs=[e.cc_out[0:1, slot * 128:slot * 128 + 128]]))
        e.cc.wait(g)
        e.cc_ld.inc(g.dma_start(
            out=e.ccr[0:1, slot * 128:slot * 128 + 128],
            in_=e.cc_out[0:1, slot * 128:slot * 128 + 128]), 16)


def bcast_scales(e, specs):
    for slot, width, vidx, st, ct, extra in specs:
        @e.block.vector
        def _(v, slot=slot):
            e.cc_ld.wait(v)
            e.ve.inc(v.tensor_reduce(e.valr[:, 0, 0:1],
                                     e.ccr[0:1, slot * 128: (slot + 1) * 128],
                                     AX.X, AO.max))

        @e.block.tensor
        def _(t):
            e.ve.wait(t)
            e.te.inc(t.matmul(e.ps_misc[:, 0:8], e.onesr_s[:],
                              e.valr[0:1, 0, 0:1].broadcast_to([1, 8]),
                              start=True, stop=True))

        @e.block.vector
        def _(v, st=st):
            e.te.wait(v)
            e.ve.inc(v.tensor_scalar(st[:], e.ps_misc[:, 0:8], INV32767, TINY,
                                     AO.mult, AO.max))

        @e.block.scalar
        def _(s, st=st, ct=ct, extra=extra):
            e.ve.wait(s)
            li = None
            if extra is not None:
                li = extra(s)
            if ct is not None:
                s.activation(ct[:], st[:], AF.Ln)
                li = s.activation(ct[:], ct[:], AF.Exp, scale=-1.0)
            if li is not None:
                e.se.inc(li)

        @e.block.vector
        def _(v):
            e.se.wait(v)
            e.ve.inc(v.tensor_copy(e.mbt[:], e.ps_misc[:, 0:8]))


def emit_ln(e, xn_bf):
    nc, block = e.nc, e.block
    with contextlib.ExitStack() as s1:
        x_s = s1.enter_context(nc.sbuf_tensor("x_s", [128, DC, L], F32))
        sqb = s1.enter_context(nc.sbuf_tensor("sqb", [128, DC, SL], F32))
        mu = s1.enter_context(nc.sbuf_tensor("mu", [1, SL], F32))
        e2 = s1.enter_context(nc.sbuf_tensor("e2", [1, SL], F32))
        va = s1.enter_context(nc.sbuf_tensor("va", [1, SL], F32))
        va2 = s1.enter_context(nc.sbuf_tensor("va2", [1, SL], F32))
        aco = s1.enter_context(nc.sbuf_tensor("aco", [1, SL], F32))
        bco = s1.enter_context(nc.sbuf_tensor("bco", [1, SL], F32))
        xnt = s1.enter_context(nc.sbuf_tensor("xnt", [128, SL], F32))

        ps_sx = s1.enter_context(nc.psum_tensor("ps_sx", [1, SL], F32))
        ps_sq = s1.enter_context(nc.psum_tensor("ps_sq", [1, SL], F32))
        ps_a = s1.enter_context(nc.psum_tensor("ps_a", [128, SL], F32))
        ps_b = s1.enter_context(nc.psum_tensor("ps_b", [128, SL], F32))

        @block.sync
        def _(sy):
            for dc in range(DC):
                e.s_ld.inc(sy.dma_start(
                    out=x_s[:, dc, :], in_=e.xT[dc * 128:(dc + 1) * 128, :]), 16)

        for lsl in range(NSL):
            cs = slice(lsl * SL, (lsl + 1) * SL)

            @block.scalar
            def _(s, cs=cs):
                e.s_ld.wait(s)
                e.te.wait(s)
                for dc in range(DC):
                    e.se.inc(s.activation(sqb[:, dc, :], x_s[:, dc, cs], AF.Square))

            @block.tensor
            def _(t, cs=cs):
                e.se.wait(t)
                e.ge.wait(t)
                e.ve.wait(t)
                for dc in range(DC):
                    t.matmul(ps_sx[:], e.onecf[:], x_s[:, dc, cs],
                             start=(dc == 0), stop=(dc == DC - 1))
                for dc in range(DC):
                    e.te.inc(t.matmul(ps_sq[:], e.onecf[:], sqb[:, dc, :], start=(dc == 0), stop=(dc == DC - 1)))

            @block.vector
            def _(v):
                e.te.wait(v)
                v.tensor_scalar(mu[:], ps_sx[:], 1.0 / D, None, AO.mult)
                v.tensor_scalar(e2[:], ps_sq[:], 1.0 / D, None, AO.mult)
                v.tensor_tensor(va[:], mu[:], mu[:], AO.mult)
                v.tensor_tensor(va[:], e2[:], va[:], AO.subtract)
                e.ve.inc(v.tensor_scalar(va[:], va[:], EPS_LN, None, AO.add))

            @block.scalar
            def _(s):
                e.ve.wait(s)
                s.activation(va2[:], va[:], AF.Ln)
                e.se.inc(s.activation(aco[:], va2[:], AF.Exp, scale=-0.5))

            @block.vector
            def _(v):
                e.se.wait(v)
                v.tensor_tensor(bco[:], mu[:], aco[:], AO.mult)
                e.ve.inc(v.tensor_scalar(bco[:], bco[:], -1.0, None, AO.mult))

            @block.tensor
            def _(t):
                e.ve.wait(t)
                t.matmul(ps_a[:], e.onesr_s[:], aco[:], start=True, stop=True)
                e.te.inc(t.matmul(ps_b[:], e.onesr_s[:], bco[:],
                                  start=True, stop=True))

            @block.vector
            def _(v, lsl=lsl, cs=cs):
                e.te.wait(v)
                for dc in range(DC):
                    v.tensor_tensor(xnt[:], x_s[:, dc, cs], ps_a[:], AO.mult)
                    v.tensor_tensor(xn_bf[:, dc, cs], xnt[:], ps_b[:], AO.add)
                    e.ve.inc(v.tensor_reduce(e.mxc[:, lsl * DC + dc: lsl * DC + dc + 1], xn_bf[:, dc, cs], AX.X, AO.max, apply_absolute_value=True))




def emit_xq(e, xn_bf, xq_bf):
    nc, block = e.nc, e.block
    with contextlib.ExitStack() as s0:
        yt = s0.enter_context(nc.sbuf_tensor("yt_xq", [128, SL], F32))

        @block.vector
        def _(v):
            e.ve.wait(v)
            e.se.wait(v)
            e.te.wait(v)
            for dc in range(DC):
                for lsl in range(NSL):
                    cs = slice(lsl * SL, (lsl + 1) * SL)
                    v.tensor_scalar(yt[:], xn_bf[:, dc, cs], e.c1t[:, 0:1],
                                    MAGIC, AO.mult, AO.add)
                    e.ve.inc(v.tensor_scalar(xq_bf[:, dc, cs], yt[:], MAGIC, e.s1t[:, 0:1], AO.subtract, AO.mult))


def emit_attention(e, xn_bf):
    nc, block = e.nc, e.block
    with contextlib.ExitStack() as s2:
        wbuf = s2.enter_context(nc.sbuf_tensor("wbuf", [128, DC, D], BF16))
        proj_bf = s2.enter_context(nc.sbuf_tensor("proj_bf", [128, DC, P], BF16))
        qkbuf = DBuf(nc, s2, "qkbuf", [128, DC, SL], BF16)
        qp_bf = s2.enter_context(nc.sbuf_tensor("qp_bf", [128, PC, L], BF16))
        kp_bf = s2.enter_context(nc.sbuf_tensor("kp_bf", [128, LC, P], BF16))
        kv_bf = s2.enter_context(nc.sbuf_tensor("kv_bf", [128, PC, D], BF16))
        zsum = s2.enter_context(nc.sbuf_tensor("zsum", [1, L], F32))
        zrec = zsum
        at = DBuf(nc, s2, "at_elu", [128, SL], F32)
        bt = DBuf(nc, s2, "bt_elu", [128, SL], F32)
        ps_q = s2.enter_context(nc.psum_tensor("ps_q", [128, 2, SL], F32))

        @block.gpsimd
        def _(g):
            e.te.wait(g)
            e.ve.wait(g)
            e.se.wait(g)
            for dc in range(DC):
                e.g_ld.inc(g.dma_start(out=proj_bf[:, dc, :],
                                       in_=e.proj[dc * 128:(dc + 1) * 128, :]), 16)

        for which, w_h in (("q", e.wqT), ("k", e.wkT)):
            @block.gpsimd
            def _(g, w_h=w_h):
                e.te.wait(g)
                e.ve.wait(g)
                for dc in range(DC):
                    e.g_ld.inc(g.dma_start(out=wbuf[:, dc, :],
                                           in_=w_h[dc * 128:(dc + 1) * 128, :]), 16)

            with contextlib.ExitStack() as sq:
                ps_p = sq.enter_context(nc.psum_tensor("ps_p_" + which,
                                                       [128, 4, SL], F32))
                ntile = PC if which == "q" else 4
                tw = SL if which == "q" else P
                se_w = e.se.n
                ve_w = e.ve.n
                PL = DC + ntile

                def evac_m(lsl, ec, se_w=se_w, PL=PL):
                    return se_w + lsl * PL + ec + 1

                def elu_m(tix, se_w=se_w, PL=PL, ntile=ntile):
                    return se_w + (tix // ntile) * PL + DC + tix % ntile + 1

                for lsl in range(NSL):
                    cs = slice(lsl * SL, (lsl + 1) * SL)
                    te0 = e.te.n

                    @block.tensor
                    def _(t, cs=cs, lsl=lsl, which=which):
                        e.g_ld.wait(t)
                        if which == "q" and lsl == 0:
                            e.ve.wait(t)
                            e.se.wait(t)
                        for ec in range(DC):
                            if ec >= 2:
                                t.wait_ge(e.se.h, evac_m(lsl, ec - 2))
                            for dc in range(DC):
                                mm = t.matmul(ps_q[:, ec % 2, :],
                                              wbuf[:, dc, ec * 128:(ec + 1) * 128],
                                              xn_bf[:, dc, cs],
                                              start=(dc == 0), stop=(dc == DC - 1))
                            e.te.inc(mm)

                    @block.scalar
                    def _(s, te0=te0, lsl=lsl):
                        for ec in range(DC):
                            s.wait_ge(e.te.h, te0 + ec + 1)
                            e.se.inc(s.activation(qkbuf[lsl][:, ec, :],
                                                  ps_q[:, ec % 2, :],
                                                  AF.Identity))

                    te1 = e.te.n

                    @block.tensor
                    def _(t, lsl=lsl, which=which):
                        for ti in range(ntile):
                            tix = lsl * ntile + ti
                            t.wait_ge(e.se.h, evac_m(lsl, DC - 1))
                            if tix >= 4:
                                t.wait_ge(e.se.h, elu_m(tix - 4))
                            if which == "q":
                                for ec in range(DC):
                                    mm = t.matmul(
                                        ps_p[:, tix % 4, :],
                                        proj_bf[:, ec, ti * 128:(ti + 1) * 128],
                                        qkbuf[lsl][:, ec, :],
                                        start=(ec == 0), stop=(ec == DC - 1))
                            else:
                                for ec in range(DC):
                                    mm = t.matmul(
                                        ps_p[:, tix % 4, 0:P],
                                        qkbuf[lsl][:, ec, ti * 128:(ti + 1) * 128],
                                        proj_bf[:, ec, :],
                                        start=(ec == 0), stop=(ec == DC - 1))
                            e.te.inc(mm)

                    @block.scalar
                    def _(s, lsl=lsl, te1=te1, tw=tw):
                        for ti in range(ntile):
                            tix = lsl * ntile + ti
                            s.wait_ge(e.te.h, te1 + ti + 1)
                            if tix >= 2:
                                s.wait_ge(e.ve.h, ve_w + tix - 1)
                            s.activation(at[tix][:, 0:tw], ps_p[:, tix % 4, 0:tw],
                                         AF.Relu)
                            s.activation(bt[tix][:, 0:tw], ps_p[:, tix % 4, 0:tw],
                                         AF.Relu, scale=-1.0)
                            e.se.inc(s.activation(bt[tix][:, 0:tw],
                                                  bt[tix][:, 0:tw],
                                                  AF.Exp, scale=-1.0))

                    @block.vector
                    def _(v, lsl=lsl, cs=cs, which=which, tw=tw):
                        for ti in range(ntile):
                            tix = lsl * ntile + ti
                            v.wait_ge(e.se.h, elu_m(tix))
                            if which == "q":
                                dst = qp_bf[:, ti, cs]
                            else:
                                dst = kp_bf[:, lsl * 4 + ti, :]
                            e.ve.inc(v.tensor_tensor(dst, at[tix][:, 0:tw],
                                                     bt[tix][:, 0:tw], AO.add))

        @block.gpsimd
        def _(g):
            e.te.wait(g)
            for dc in range(DC):
                e.g_ld.inc(g.dma_start(out=wbuf[:, dc, :],
                                       in_=e.wvT[dc * 128:(dc + 1) * 128, :]), 16)

        with contextlib.ExitStack() as sv:
            vbuf = DBuf(nc, sv, "vbuf", [128, SL], BF16)
            ps_kv = sv.enter_context(nc.psum_tensor("ps_kv", [128, 4, SL], F32))
            te_v = e.te.n
            se_v = e.se.n

            @block.tensor
            def _(t):
                e.g_ld.wait(t)
                e.ve.wait(t)
                for j in range(LC * 2):
                    lc, dsl = j // 2, j % 2
                    if j >= 2:
                        t.wait_ge(e.se.h, se_v + j - 1)
                    for dc in range(DC):
                        mm = t.matmul(ps_q[:, j % 2, :],
                                      xn_bf[:, dc, lc * 128:(lc + 1) * 128],
                                      wbuf[:, dc, dsl * SL:(dsl + 1) * SL],
                                      start=(dc == 0), stop=(dc == DC - 1))
                    e.te.inc(mm)
                    if j >= 1:
                        pj = j - 1
                        plc, pdsl = pj // 2, pj % 2
                        t.wait_ge(e.se.h, se_v + pj + 1)
                        for pc in range(PC):
                            t.matmul(ps_kv[:, pc * 2 + pdsl, :],
                                     kp_bf[:, plc, pc * 128:(pc + 1) * 128],
                                     vbuf[pj % 2],
                                     start=(plc == 0), stop=(plc == LC - 1))
                pj = LC * 2 - 1
                plc, pdsl = pj // 2, pj % 2
                t.wait_ge(e.se.h, se_v + pj + 1)
                for pc in range(PC):
                    mm = t.matmul(ps_kv[:, pc * 2 + pdsl, :],
                                  kp_bf[:, plc, pc * 128:(pc + 1) * 128],
                                  vbuf[pj % 2],
                                  start=(plc == 0), stop=(plc == LC - 1))
                e.te.inc(mm)

            @block.scalar
            def _(s):
                for j in range(LC * 2):
                    s.wait_ge(e.te.h, te_v + j + 1)
                    e.se.inc(s.activation(vbuf[j % 2], ps_q[:, j % 2, :],
                                          AF.Identity))
                s.wait_ge(e.te.h, te_v + LC * 2 + 1)
                for pc in range(PC):
                    for dsl in range(2):
                        e.se.inc(s.activation(
                            kv_bf[:, pc, dsl * SL:(dsl + 1) * SL],
                            ps_kv[:, pc * 2 + dsl, :], AF.Identity))

        with contextlib.ExitStack() as sz:
            ps_z = sz.enter_context(nc.psum_tensor("ps_z", [1, SL], F32))

            for lsl in range(NSL):
                cs = slice(lsl * SL, (lsl + 1) * SL)

                @block.tensor
                def _(t, cs=cs):
                    e.ve.wait(t)
                    e.se.wait(t)
                    for pc in range(PC):
                        e.te.inc(t.matmul(ps_z[:], e.onecb[:], qp_bf[:, pc, cs], start=(pc == 0), stop=(pc == PC - 1)))

                @block.vector
                def _(v, cs=cs):
                    e.te.wait(v)
                    e.ve.inc(v.tensor_scalar(zsum[0:1, cs], ps_z[:], EPS_Z,
                                             None, AO.add))

            @block.scalar
            def _(s):
                e.ve.wait(s)
                s.activation(zrec[:], zsum[:], AF.Ln)
                e.se.inc(s.activation(zrec[:], zrec[:], AF.Exp, scale=-1.0))

            @block.vector
            def _(v):
                e.se.wait(v)
                e.ve.inc(v.tensor_copy(e.vrow[0:1, 0:8], zrec[0:1, 0:8]))

        with contextlib.ExitStack() as sn:
            zbs = sn.enter_context(nc.sbuf_tensor("zbs", [128, NSL, SL], F32))
            xrl = DBuf(nc, sn, "xrl", [128, SL], F32)
            xas = DBuf(nc, sn, "xas", [128, SL], F32)
            ps_zb = sn.enter_context(nc.psum_tensor("ps_zb", [128, 2, SL], F32))
            se_z = e.se.n
            te_z = e.te.n

            @block.tensor
            def _(t):
                e.ve.wait(t)
                for lsl in range(NSL):
                    cs = slice(lsl * SL, (lsl + 1) * SL)
                    if lsl >= 2:
                        t.wait_ge(e.se.h, se_z + lsl - 1)
                    e.te.inc(t.matmul(ps_zb[:, lsl % 2, :], e.onesr_s[:],
                                      zrec[0:1, cs], start=True, stop=True))

            @block.scalar
            def _(s):
                for lsl in range(NSL):
                    s.wait_ge(e.te.h, te_z + lsl + 1)
                    e.se.inc(s.activation(zbs[:, lsl, :], ps_zb[:, lsl % 2, :],
                                          AF.Identity))

            NJ = DC * NSL
            te_base = e.te.n
            ve_base = e.ve.n
            xrl_st = Stream(e, "xrlS")
            xas_st = Stream(e, "xasS")
            e.store_streams.append(xas_st)

            @block.sync
            def _(sy, ve_base=ve_base):
                e.te.wait(sy)
                e.ve.wait(sy)
                e.se.wait(sy)
                for j in range(NJ):
                    dc, lsl = j // NSL, j % NSL
                    if j >= 2:
                        sy.wait_ge(e.ve.h, ve_base + j - 1)
                    xrl_st.inc(sy.dma_start(
                        out=xrl[j % 2],
                        in_=e.xT[dc * 128:(dc + 1) * 128,
                                 lsl * SL:(lsl + 1) * SL]), j)

            @block.tensor
            def _(t, ve_base=ve_base):
                e.se.wait(t)
                for j in range(NJ):
                    dc, lsl = j // NSL, j % NSL
                    cs = slice(lsl * SL, (lsl + 1) * SL)
                    if j >= 2:
                        t.wait_ge(e.ve.h, ve_base + j - 1)
                    for pc in range(PC):
                        mm = t.matmul(ps_q[:, j % 2, :],
                                      kv_bf[:, pc, dc * 128:(dc + 1) * 128],
                                      qp_bf[:, pc, cs],
                                      start=(pc == 0), stop=(pc == PC - 1))
                    e.te.inc(mm)

            @block.vector
            def _(v, te_base=te_base):
                for j in range(NJ):
                    dc, lsl = j // NSL, j % NSL
                    v.wait_ge(e.te.h, te_base + j + 1)
                    xrl_st.wait_done(v, j)
                    if j >= 2:
                        xas_st.wait_done(v, j - 2)
                    v.tensor_tensor(xas[j % 2], ps_q[:, j % 2, :],
                                    zbs[:, lsl, :], AO.mult)
                    e.ve.inc(v.tensor_tensor(xas[j % 2], xas[j % 2], xrl[j % 2], AO.add))

            @block.gpsimd
            def _(g, ve_base=ve_base):
                for j in range(NJ):
                    dc, lsl = j // NSL, j % NSL
                    g.wait_ge(e.ve.h, ve_base + j + 1)
                    xas_st.inc(g.dma_start(
                        out=e.xatt_h[dc][:, lsl * SL:(lsl + 1) * SL],
                        in_=xas[j % 2]), j)


class WdqLoads:

    def __init__(self, e, wn_src, ncn, wnb):
        self.e = e
        self.wn_src = wn_src
        self.ncn = ncn
        self.wnb = wnb
        self.st = Stream(e, uniq("wnbS"), nslots=wnb.n)
        self.dqsem = Sem(e.nc, e.es, uniq("dqs"))
        self.trsem = Sem(e.nc, e.es, uniq("trs"))

    def emit(self, lo, hi):
        e, wnb, wn_src = self.e, self.wnb, self.wn_src
        st, dqsem = self.st, self.dqsem

        @e.block.sync
        def _(sy):
            for ic in range(lo, hi):
                if ic >= wnb.n:
                    sy.wait_ge(dqsem.h, ic - wnb.n + 1)
                st.inc(sy.dma_start(out=wnb[ic], in_=wn_src(ic)), ic)


def emit_wdq(e, ldstate, ncn, nb, st_all, rc_all, st_act, wnb, wnat, dqt,
             ps_tr, dest, tpc, prev_se):
    nc, block = e.nc, e.block
    wnb_st, dqsem, trsem = ldstate.st, ldstate.dqsem, ldstate.trsem
    vA = e.ve.n
    seA = e.se.n
    sc_marks = []
    evac_marks = []

    def ts_mark(c):
        return vA + c + 1

    @block.scalar
    def _(s):
        def evacs(ic):
            s.wait_ge(trsem.h, trsem.n + (ic + 1) * tpc)
            e.se.inc(s.activation(
                dest(ic), ps_tr[:, ic % 2, 0:1024]
                .rearrange("p (h i) -> p h i", h=tpc), AF.Identity))
            evac_marks.append(e.se.n)

        for ic in range(ncn):
            s.wait_ge(e.ve.h, ts_mark(ic))
            if ic >= 2:
                s.wait_ge(dqsem.h, dqsem.n + ic - 1)
            s.activation(rc_all[:, ic, :], st_all[:, ic, :], AF.Ln)
            s.activation(rc_all[:, ic, :], rc_all[:, ic, :], AF.Exp,
                         scale=-1.0)
            e.se.inc(s.copy(st_act[ic], st_all[:, ic, :]))
            sc_marks.append(e.se.n)
            if ic >= 2:
                evacs(ic - 2)
        for ic in range(max(ncn - 2, 0), ncn):
            evacs(ic)

    @block.vector
    def _(v):
        v.wait_ge(e.se.h, prev_se)

        def dq(ic):
            v.wait_ge(e.se.h, sc_marks[ic])
            if ic >= 2:
                v.wait_ge(trsem.h, trsem.n + (ic - 1) * tpc)
            v.tensor_tensor(dqt[0].rearrange("p (g b) -> p g b", b=32),
                            wnb[ic].rearrange("p (g b) -> p g b", b=32),
                            rc_all[:, ic, :].rearrange("p (g o) -> p g o", o=1)
                            .broadcast_to([128, nb, 32]), AO.mult)
            v.tensor_scalar(dqt[0], dqt[0], MAGIC, MAGIC, AO.add, AO.subtract)
            dqsem.inc(v.tensor_tensor(
                wnat[ic].rearrange("p (g b) -> p g b", b=32),
                dqt[0].rearrange("p (g b) -> p g b", b=32),
                st_act[ic].rearrange("p (g o) -> p g o", o=1)
                .broadcast_to([128, nb, 32]), AO.mult))

        for ic in range(ncn):
            wnb_st.wait_done(v, ic)
            v.tensor_reduce(st_all[:, ic, :],
                            wnb[ic].rearrange("p (g b) -> p g b", b=32),
                            AX.X, AO.max, apply_absolute_value=True)
            if ic >= 1:
                e.ve.inc(v.tensor_scalar(st_all[:, ic - 1, :],
                                         st_all[:, ic - 1, :],
                                         INV7, TINY, AO.mult, AO.max))
            if ic >= 2:
                dq(ic - 2)
        e.ve.inc(v.tensor_scalar(st_all[:, ncn - 1, :],
                                 st_all[:, ncn - 1, :],
                                 INV7, TINY, AO.mult, AO.max))
        for ic in range(max(ncn - 2, 0), ncn):
            dq(ic)

    @block.tensor
    def _(t):
        for ic in range(ncn):
            for h in range(tpc):
                if h == 0:
                    t.wait_ge(dqsem.h, dqsem.n - ncn + ic + 1)
                    if ic >= 2:
                        t.wait_ge(e.se.h, evac_marks[ic - 2])
                    else:
                        t.wait_ge(e.se.h, seA)
                trsem.inc(t.transpose(ps_tr[:, ic % 2, h * 128:(h + 1) * 128],
                                      wnat[ic][:, h * 128:(h + 1) * 128],
                                      e.ident_bf[:]))


def emit_ffn_f1(e, xq_bf, wdq, spill_h):
    nc, block = e.nc, e.block
    with contextlib.ExitStack() as sf:
        stg = DBuf(nc, sf, "stg_f1", [128, L], F16)
        ps_mm = sf.enter_context(nc.psum_tensor(uniq("ps_mm_f1"), [128, 4, SL], F32))

        te_base = e.te.n
        ve_base = e.ve.n
        se_base = e.se.n
        stg_st = Stream(e, "stgS")
        e.store_streams.append(stg_st)

        @block.tensor
        def _(t):
            e.ve.wait(t)
            e.se.wait(t)
            for j in range(IC * NSL):
                ic, lsl = j // NSL, j % NSL
                cs = slice(lsl * SL, (lsl + 1) * SL)
                if j >= 4:
                    t.wait_ge(e.se.h, se_base + j - 3)
                for hc in range(DC):
                    mm = t.matmul(ps_mm[:, j % 4, :],
                                  wdq[:, hc, ic * 128:(ic + 1) * 128],
                                  xq_bf[:, hc, cs],
                                  start=(hc == 0), stop=(hc == DC - 1))
                e.te.inc(mm)

        @block.scalar
        def _(s):
            for j in range(IC * NSL):
                ic, lsl = j // NSL, j % NSL
                cs = slice(lsl * SL, (lsl + 1) * SL)
                s.wait_ge(e.te.h, te_base + j + 1)
                if lsl == 0:
                    if ic >= 2:
                        stg_st.wait_done(s, ic - 2)
                        s.wait_ge(e.ve.h, ve_base + ic - 1)
                    else:
                        wait_stores(e, s)
                e.se.inc(s.activation(stg[ic % 2][:, cs], ps_mm[:, j % 4, :],
                                      AF.Identity))

        @block.vector
        def _(v):
            for ic in range(IC):
                v.wait_ge(e.se.h, se_base + (ic + 1) * NSL)
                e.ve.inc(v.tensor_reduce(e.mxc[:, ic:ic + 1], stg[ic % 2],
                                         AX.X, AO.max,
                                         apply_absolute_value=True))

        @block.sync
        def _(sy):
            for ic in range(IC):
                sy.wait_ge(e.se.h, se_base + (ic + 1) * NSL)
                stg_st.inc(sy.dma_start(out=spill_h[ic], in_=stg[ic % 2]), ic)


def emit_sweep1(e):
    nc, block = e.nc, e.block
    with contextlib.ExitStack() as s:
        g16 = DBuf(nc, s, "g16", [128, L], F16)
        t1 = DBuf(nc, s, "sw_t1", [128, L], F32)
        t2 = DBuf(nc, s, "sw_t2", [128, L], F32)
        t3 = DBuf(nc, s, "sw_t3", [128, L], F32)
        gst = DBuf(nc, s, "gst", [128, L], F16)

        se0 = e.se.n
        ge0 = e.ge.n
        ve0 = e.ve.n
        g16_st = Stream(e, "g16S")
        gst_st = Stream(e, "gstS")
        pre1_marks = []
        tsd_marks = []
        red_marks = []

        def act_mark(ic):
            return se0 + ic + 1

        def tt_mark(p):
            return ge0 + p + 1

        def pre1_arith(ic):
            return ve0 + (ic + 1 if ic < 2 else 3 * ic - 2)

        @block.sync
        def _(sy):
            wait_stores(e, sy)
            for ic in range(IC):
                if ic >= 2:
                    sy.wait_ge(e.ve.h, pre1_arith(ic - 2))
                g16_st.inc(sy.dma_start(out=g16[ic], in_=e.gate_h[ic]), ic)

        e.store_streams.append(gst_st)

        @block.vector
        def _(v):
            wait_stores(e, v)

            for ic in range(IC):
                g16_st.wait_done(v, ic)
                if ic >= 2:
                    v.wait_ge(e.ge.h, tt_mark(ic - 2))
                e.ve.inc(v.tensor_scalar(t1[ic], g16[ic], e.c3t[:, 0:1],
                                         MAGIC, AO.mult, AO.add))
                pre1_marks.append(e.ve.n)
                if ic >= 1:
                    p = ic - 1
                    v.wait_ge(e.se.h, act_mark(p))
                    e.ve.inc(v.tensor_scalar(t1[p], t3[p], MAGIC65535, MAGIC,
                                             AO.min, AO.subtract))
                    tsd_marks.append(e.ve.n)
                if ic >= 2:
                    p = ic - 2
                    v.wait_ge(e.ge.h, tt_mark(p))
                    e.ve.inc(v.tensor_reduce(e.mxc[:, p:p + 1], gst[p],
                                             AX.X, AO.max,
                                             apply_absolute_value=True))
                    red_marks.append(e.ve.n)
            p = IC - 1
            v.wait_ge(e.se.h, act_mark(p))
            e.ve.inc(v.tensor_scalar(t1[p], t3[p], MAGIC65535, MAGIC,
                                     AO.min, AO.subtract))
            tsd_marks.append(e.ve.n)
            for p in (IC - 2, IC - 1):
                v.wait_ge(e.ge.h, tt_mark(p))
                e.ve.inc(v.tensor_reduce(e.mxc[:, p:p + 1], gst[p],
                                         AX.X, AO.max,
                                         apply_absolute_value=True))
                red_marks.append(e.ve.n)

        @block.scalar
        def _(s_):
            for ic in range(IC):
                s_.wait_ge(e.ve.h, pre1_marks[ic])
                if ic >= 2:
                    s_.wait_ge(e.ve.h, tsd_marks[ic - 2])
                    s_.wait_ge(e.ge.h, tt_mark(ic - 2))
                s_.activation(t2[ic], t1[ic], AF.Identity,
                              bias=e.nmagict[:, 0:1])
                s_.activation(t2[ic], t2[ic], AF.Identity,
                              scale=e.kkt[:, 0:1])
                s_.activation(t3[ic], t2[ic], AF.Sigmoid, scale=65536.0)
                e.se.inc(s_.activation(t3[ic], t3[ic], AF.Identity,
                                       bias=e.magict[:, 0:1], scale=65536.0))

        @block.gpsimd
        def _(g):
            for p in range(IC):
                g.wait_ge(e.ve.h, tsd_marks[p])
                if p >= 2:
                    g.wait_ge(e.ve.h, red_marks[p - 2])
                    gst_st.wait_done(g, p - 2)
                e.ge.inc(g.tensor_tensor(gst[p], t2[p], t1[p], AO.mult))
                gst_st.inc(g.dma_start(out=e.gs_h[p], in_=gst[p]), p)


def emit_sweep2(e):
    nc, block = e.nc, e.block
    with contextlib.ExitStack() as s:
        u16 = DBuf(nc, s, "u16", [128, L], F16)
        g16 = DBuf(nc, s, "gs16", [128, L], F16)
        t1 = DBuf(nc, s, "s2_t1", [128, L], F32)
        t2 = DBuf(nc, s, "s2_t2", [128, L], F32)
        ost = DBuf(nc, s, "ost", [128, L], F16)

        se0 = e.se.n
        ge0 = e.ge.n
        ve0 = e.ve.n
        u16_st = Stream(e, "u16S")
        gs16_st = Stream(e, "gs16S")
        ost_st = Stream(e, "ostS")
        gs_marks = []
        red_marks = []

        def act_mark(ic):
            return se0 + ic + 1

        def tt_mark(p):
            return ge0 + p + 1

        def gs_arith(ic):
            return ve0 + (ic + 1 if ic < 2 else 2 * ic - 1)

        @block.sync
        def _(sy):
            wait_stores(e, sy)
            for ic in range(IC):
                if ic >= 2:
                    sy.wait_ge(e.se.h, act_mark(ic - 2))
                    sy.wait_ge(e.ve.h, gs_arith(ic - 2))
                u16_st.inc(sy.dma_start(out=u16[ic], in_=e.up_h[ic]), ic)
                gs16_st.inc(sy.dma_start(out=g16[ic], in_=e.gs_h[ic]), ic)

        e.store_streams.append(ost_st)

        @block.vector
        def _(v):
            wait_stores(e, v)
            for ic in range(IC):
                gs16_st.wait_done(v, ic)
                if ic >= 2:
                    v.wait_ge(e.ge.h, tt_mark(ic - 2))
                v.tensor_scalar(t2[ic], g16[ic], e.c4t[:, 0:1], MAGIC,
                                AO.mult, AO.add)
                e.ve.inc(v.tensor_scalar(t2[ic], t2[ic], MAGIC, e.s4t[:, 0:1],
                                         AO.subtract, AO.mult))
                gs_marks.append(e.ve.n)
                if ic >= 2:
                    p = ic - 2
                    v.wait_ge(e.ge.h, tt_mark(p))
                    e.ve.inc(v.tensor_reduce(e.mxc[:, p:p + 1], ost[p],
                                             AX.X, AO.max,
                                             apply_absolute_value=True))
                    red_marks.append(e.ve.n)
            for p in (IC - 2, IC - 1):
                v.wait_ge(e.ge.h, tt_mark(p))
                e.ve.inc(v.tensor_reduce(e.mxc[:, p:p + 1], ost[p],
                                         AX.X, AO.max,
                                         apply_absolute_value=True))
                red_marks.append(e.ve.n)

        @block.scalar
        def _(s_):
            for ic in range(IC):
                u16_st.wait_done(s_, ic)
                if ic >= 2:
                    s_.wait_ge(e.ge.h, tt_mark(ic - 2))
                s_.activation(t1[ic], u16[ic], AF.Identity,
                              scale=e.c2t[:, 0:1], bias=e.magict[:, 0:1])
                s_.activation(t1[ic], t1[ic], AF.Identity,
                              bias=e.nmagict[:, 0:1])
                e.se.inc(s_.activation(t1[ic], t1[ic], AF.Identity,
                                       scale=e.s2t[:, 0:1]))

        @block.gpsimd
        def _(g):
            for p in range(IC):
                g.wait_ge(e.ve.h, gs_marks[p])
                g.wait_ge(e.se.h, act_mark(p))
                if p >= 2:
                    g.wait_ge(e.ve.h, red_marks[p - 2])
                    ost_st.wait_done(g, p - 2)
                e.ge.inc(g.tensor_tensor(ost[p], t1[p], t2[p], AO.mult))
                ost_st.inc(g.dma_start(out=e.gs_h[p], in_=ost[p]), p)


def emit_sweep3(e, wdq_dn):
    nc, block = e.nc, e.block
    with contextlib.ExitStack() as s:
        ob16 = DBuf(nc, s, "ob16", [128, IC, SL], F16, nslots=1)
        obf = DBuf(nc, s, "obf", [128, IC, SL], BF16)
        yt = DBuf(nc, s, "yt_s3", [128, SL], F32)
        xrl2 = DBuf(nc, s, "xrl2", [128, SL], F32)
        ostg = DBuf(nc, s, "ostg", [128, SL], F32)
        ps_dn = s.enter_context(nc.psum_tensor("ps_dn", [128, 2, SL], F32))
        ob16_st = Stream(e, "ob16S")
        xrl2_st = Stream(e, "xrl2S")
        ostg_st = Stream(e, "ostgS")
        e.store_streams.append(ostg_st)

        se0 = e.se.n
        te0 = e.te.n
        conv_marks = {}
        ostg_marks = {}

        def act_mark(j):
            return se0 + j + 1

        def mm_mark(lsl, hc):
            return te0 + lsl * DC + hc + 1

        n = e.ve.n
        for lsl in range(NSL):
            for ic in range(IC):
                n += 1
                conv_marks[(lsl, ic)] = n
            if lsl >= 1:
                for hc in range(DC):
                    n += 1
                    ostg_marks[(lsl - 1, hc)] = n
        for hc in range(DC):
            n += 1
            ostg_marks[(NSL - 1, hc)] = n

        @block.scalar
        def _(s_):
            for lsl in range(NSL):
                ob16_st.wait_done(s_, lsl)
                for ic in range(IC):
                    j = lsl * IC + ic
                    if j >= 2:
                        s_.wait_ge(e.ve.h, conv_marks[divmod(j - 2, IC)])
                    s_.activation(yt[j], ob16[lsl][:, ic, :], AF.Identity,
                                  scale=e.c5t[:, 0:1], bias=e.magict[:, 0:1])
                    e.se.inc(s_.activation(yt[j], yt[j], AF.Identity,
                                           bias=e.nmagict[:, 0:1]))

        @block.vector
        def _(v):
            wait_stores(e, v)

            def convs(lsl):
                for ic in range(IC):
                    j = lsl * IC + ic
                    v.wait_ge(e.se.h, act_mark(j))
                    if lsl >= 2 and ic == 0:
                        v.wait_ge(e.te.h, mm_mark(lsl - 2, DC - 1))
                    e.ve.inc(v.tensor_scalar(obf[lsl][:, ic, :], yt[j],
                                             e.s5t[:, 0:1], None, AO.mult))

            def ostg_row(lsl):
                for hc in range(DC):
                    g = lsl * DC + hc
                    v.wait_ge(e.te.h, mm_mark(lsl, hc))
                    xrl2_st.wait_done(v, g)
                    if g >= 2:
                        ostg_st.wait_done(v, g - 2)
                    e.ve.inc(v.tensor_tensor(ostg[g], ps_dn[:, hc % 2, :],
                                             xrl2[g], AO.add))

            for lsl in range(NSL):
                convs(lsl)
                if lsl >= 1:
                    ostg_row(lsl - 1)
            ostg_row(NSL - 1)

        @block.sync
        def _(sy):
            wait_stores(e, sy)
            for lsl in range(NSL):
                if lsl >= 1:
                    sy.wait_ge(e.se.h, act_mark(lsl * IC - 1))
                ob16_st.inc(sy.dma_start(
                    out=ob16[lsl],
                    in_=e.gs_h[:, :, lsl * SL:(lsl + 1) * SL]
                    .rearrange("i p n -> p i n")), lsl)

        @block.tensor
        def _(t):
            t.wait_ge(e.se.h, se0)
            for lsl in range(NSL):
                cs = slice(lsl * SL, (lsl + 1) * SL)
                for hc in range(DC):
                    g = lsl * DC + hc
                    if hc == 0:
                        t.wait_ge(e.ve.h, conv_marks[(lsl, IC - 1)])
                    if g >= 2:
                        t.wait_ge(e.ve.h, ostg_marks[divmod(g - 2, DC)])
                    for ic in range(IC):
                        mm = t.matmul(ps_dn[:, hc % 2, :],
                                      wdq_dn[:, ic, hc * 128:(hc + 1) * 128],
                                      obf[lsl][:, ic, :],
                                      start=(ic == 0), stop=(ic == IC - 1))
                    e.te.inc(mm)

        NJ = NSL * DC

        def store_out(g_, g):
            lsl, hc = divmod(g, DC)
            g_.wait_ge(e.ve.h, ostg_marks[(lsl, hc)])
            ostg_st.inc(g_.dma_start(
                out=e.out[hc * 128:(hc + 1) * 128,
                          lsl * SL:(lsl + 1) * SL],
                in_=ostg[g]), g)

        @block.gpsimd
        def _(g_):
            wait_stores(e, g_)
            for g in range(NJ):
                if g >= 2:
                    store_out(g_, g - 2)
                lsl, hc = divmod(g, DC)
                xrl2_st.inc(g_.dma_start(
                    out=xrl2[g],
                    in_=e.xatt_h[hc][:, lsl * SL:(lsl + 1) * SL]), g)
            store_out(g_, NJ - 2)
            store_out(g_, NJ - 1)



def emit_program(e):
    nc, block = e.nc, e.block

    @block.sync
    def _(sy):
        e.s_ld.inc(sy.dma_start(out=e.onesr_s[:], in_=e.onesr[:]), 16)

    @block.tensor
    def _(t):
        e.s_ld.wait(t)
        e.te.inc(t.matmul(e.ps_misc[:, 0:128], e.onesr_s[:], e.onesr_s[:],
                          start=True, stop=True))

    @block.gpsimd
    def _(g):
        e.g_ld.inc(g.dma_start(out=e.ident_bf[:], in_=e.ident[:]), 16)
        e.ge.inc(g.memset(e.onecf[:], 1.0))
        e.ge.inc(g.memset(e.onecb[:], 1.0))
        e.ge.inc(g.memset(e.mloc[:], 0.0))
        e.ge.inc(g.memset(e.magict[:], MAGIC))
        e.ge.inc(g.memset(e.nmagict[:], -MAGIC))

    def set_kk(s):
        return s.activation(e.kkt[:], e.s3t[:], AF.Identity, scale=K216)

    with contextlib.ExitStack() as sc:
        wdq = sc.enter_context(nc.sbuf_tensor("wdq", [128, DC * I], BF16))
        wdq_ud = wdq[:].rearrange("p (h i) -> p h i", h=DC)
        wdq_dn = wdq[:].rearrange("p (i h) -> p i h", i=IC)
        wnb = DBuf(nc, sc, "wnb", [128, D], F32, nslots=3)
        st_all = sc.enter_context(nc.sbuf_tensor("st_all", [128, IC, 32], F32))
        rc_all = sc.enter_context(nc.sbuf_tensor("rc_all", [128, IC, 32], F32))
        st_act = DBuf(nc, sc, "st_act", [128, 32], F32)
        wnat = DBuf(nc, sc, "wnat", [128, D], BF16)
        dqt = DBuf(nc, sc, "dqt", [128, D], F32, nslots=1)
        dest_ud = lambda ic: wdq_ud[:, 0:DC, ic * 128:(ic + 1) * 128]

        xn_bf = sc.enter_context(nc.sbuf_tensor("xn_bf", [128, DC, L], BF16))
        emit_ln(e, xn_bf)
        ld_gate = WdqLoads(
            e, lambda ic: e.wgate_n[ic * 128:(ic + 1) * 128, :], IC, wnb)
        ld_gate.emit(0, wnb.n)
        emit_attention(e, xn_bf)
        ld_gate.emit(wnb.n, IC)

        stage_maxcols_to_cc(e, e.mxc[:, 0:32], 32, 0)
        collective_max(e, 1024, 0)

        ps_tr = sc.enter_context(nc.psum_tensor("ps_tr", [128, 2, 1024],
                                                BF16))
        prev_se = e.se.n
        emit_wdq(e, ld_gate, IC, 32, st_all, rc_all, st_act, wnb, wnat,
                 dqt, ps_tr, dest_ud, DC, prev_se)

        bcast_scales(e, [(0, 1024, 0, e.s1t, e.c1t, None)])
        xq_bf = xn_bf
        emit_xq(e, xn_bf, xq_bf)

        emit_ffn_f1(e, xq_bf, wdq_ud, e.gate_h)
        stage_maxcols_to_cc(e, e.mxc[:, 0:IC], IC, 1)
        collective_max(e, 1024, 1)
        ld_up = WdqLoads(
            e, lambda ic: e.wup_n[ic * 128:(ic + 1) * 128, :], IC, wnb)
        ld_up.emit(0, IC)
        prev_se = e.se.n
        emit_wdq(e, ld_up, IC, 32, st_all, rc_all, st_act, wnb, wnat,
                 dqt, ps_tr, dest_ud, DC, prev_se)
        bcast_scales(e, [(1, 1024, 1, e.s3t, e.c3t, set_kk)])

        emit_ffn_f1(e, xq_bf, wdq_ud, e.up_h)
        stage_maxcols_to_cc(e, e.mxc[:, 0:IC], IC, 0)
        collective_max(e, 1024, 0)

        emit_sweep1(e)
        ld_dn = WdqLoads(
            e, lambda ic: e.wdown_n[(ic // 3) * 128:(ic // 3 + 1) * 128,
                                    (ic % 3) * 1024:(ic % 3 + 1) * 1024],
            IC, wnb)
        ld_dn.emit(0, IC)
        stage_maxcols_to_cc(e, e.mxc[:, 0:IC], IC, 1)
        collective_max(e, 1024, 1)
        bcast_scales(e, [(0, 1024, 0, e.s2t, e.c2t, None)])

        prev_se = e.se.n
        emit_wdq(e, ld_dn, IC, 32, st_all, rc_all, st_act, wnb, wnat,
                 dqt, ps_tr,
                 lambda ic: wdq_dn[:, (ic % 3) * 8:(ic % 3) * 8 + 8,
                                   (ic // 3) * 128:(ic // 3 + 1) * 128],
                 DC, prev_se)
        bcast_scales(e, [(1, 1024, 1, e.s4t, e.c4t, None)])

        emit_sweep2(e)
        stage_maxcols_to_cc(e, e.mxc[:, 0:IC], IC, 0)
        collective_max(e, 1024, 0)
        bcast_scales(e, [(0, 1024, 0, e.s5t, e.c5t, None)])

        emit_sweep3(e, wdq_dn)

    @block.sync
    def _(sy):
        wait_stores(e, sy)



def kernel(L0_attn_x, L0_attn_Wq, L0_attn_Wk, L0_attn_Wv, L0_attn_proj,
           L0_ffn_w_gate, L0_ffn_w_up, L0_ffn_w_down):
    X = np.ascontiguousarray(np.asarray(L0_attn_x, dtype=np.float32))
    Wq = np.asarray(L0_attn_Wq, dtype=np.float32)
    Wk = np.asarray(L0_attn_Wk, dtype=np.float32)
    Wv = np.asarray(L0_attn_Wv, dtype=np.float32)
    proj = np.ascontiguousarray(np.asarray(L0_attn_proj, dtype=np.float32))
    w_gate = np.ascontiguousarray(np.asarray(L0_ffn_w_gate, dtype=np.float32))
    w_up = np.ascontiguousarray(np.asarray(L0_ffn_w_up, dtype=np.float32))
    w_down = np.ascontiguousarray(np.asarray(L0_ffn_w_down, dtype=np.float32))

    wqT = np.ascontiguousarray(Wq.T)
    wkT = np.ascontiguousarray(Wk.T)
    wvT = np.ascontiguousarray(Wv.T)

    ident = np.eye(128, dtype=np.float32)
    onesr = np.ones((1, 128), dtype=np.float32)

    nc = build_nc()
    in_maps = []
    for b in range(B):
        in_maps.append({
            "xT": np.ascontiguousarray(X[b].T),
            "wqT": wqT, "wkT": wkT, "wvT": wvT, "proj": proj,
            "wup_n": w_up, "wgate_n": w_gate, "wdown_n": w_down,
            "ident": ident, "onesr": onesr,
        })
    res = run_bass_kernel_spmd(nc, in_maps, core_ids=list(range(B)))
    out = np.stack([np.asarray(res.results[b]["out"]).T for b in range(B)], axis=0)
    return np.ascontiguousarray(out, dtype=np.float32)


# version 14
# speedup vs baseline: 1.1020x; 1.1020x over previous
import contextlib
import numpy as np
import concourse.bass as bass
import concourse.mybir as mybir
from concourse.bass_utils import run_bass_kernel_spmd

F32 = mybir.dt.float32
BF16 = mybir.dt.bfloat16
F16 = mybir.dt.float16
AO = mybir.AluOpType
AF = mybir.ActivationFunctionType
AX = mybir.AxisListType

MAGIC = 12582912.0
MAGIC65535 = MAGIC + 65535.0
INV7 = 1.0 / 7.0
INV32767 = 1.0 / 32767.0
TINY = 1e-12
EPS_LN = 1e-5
EPS_Z = 1e-8
K216 = 2.0 ** -16

B, L, D, P, I = 8, 2048, 1024, 256, 3072
DC, IC, PC, LC = 8, 24, 2, 16
NSL, SL = 4, 512


class Sem:
    def __init__(self, nc, es, name):
        self.h = es.enter_context(nc.semaphore(name))
        self.n = 0

    def inc(self, instr, k=1):
        instr.then_inc(self.h, k)
        self.n += k
        return instr

    def wait(self, eng, val=None):
        eng.wait_ge(self.h, self.n if val is None else val)


class Env:
    pass


_UNIQ = [0]


def uniq(name):
    _UNIQ[0] += 1
    return f"{name}_u{_UNIQ[0]}"


class Stream:

    def __init__(self, e, name, nslots=2):
        self.sems = [Sem(e.nc, e.es, uniq(f"{name}{i}")) for i in range(nslots)]
        self.nslots = nslots

    def inc(self, instr, j):
        self.sems[j % self.nslots].inc(instr, 16)

    def wait_done(self, eng, j):
        s = self.sems[j % self.nslots]
        eng.wait_ge(s.h, 16 * ((j // self.nslots) + 1))

    def wait_all(self, eng):
        for s in self.sems:
            if s.n:
                eng.wait_ge(s.h, s.n)


class DBuf:
    def __init__(self, nc, es, name, shape, dtype, nslots=2):
        self.t = [es.enter_context(nc.sbuf_tensor(uniq(f"{name}_{i}"), shape, dtype))
                  for i in range(nslots)]
        self.n = nslots

    def __getitem__(self, i):
        return self.t[i % self.n][:]



def build_nc():
    nc = bass.Bass("TRN2", target_bir_lowering=False, debug=False,
                   detect_race_conditions=False)
    e = Env()
    e.nc = nc

    def din(name, shape, dtype=F32):
        return nc.dram_tensor(name, shape, dtype, kind="ExternalInput").ap()

    e.xT = din("xT", [D, L])
    e.wqT = din("wqT", [D, D])
    e.wkT = din("wkT", [D, D])
    e.wvT = din("wvT", [D, D])
    e.proj = din("proj", [D, P])
    e.wup_n = din("wup_n", [I, D])
    e.wgate_n = din("wgate_n", [I, D])
    e.wdown_n = din("wdown_n", [D, I])
    e.ident = din("ident", [128, 128])
    e.onesr = din("onesr", [1, 128])

    e.out = nc.dram_tensor("out", [D, L], F32, kind="ExternalOutput").ap()

    e.xatt_h = nc.dram_tensor("xatt_sp", [DC, 128, L], F32, kind="Internal").ap()
    e.gate_h = nc.dram_tensor("gate_sp", [IC, 128, L], F16, kind="Internal").ap()
    e.up_h = nc.dram_tensor("up_sp", [IC, 128, L], F16, kind="Internal").ap()
    e.gs_h = nc.dram_tensor("gs_sp", [IC, 128, L], F16, kind="Internal").ap()
    e.cc_in = nc.dram_tensor("cc_in", [1, 128], F32, kind="Internal").ap()
    e.cc_out = nc.dram_tensor("cc_out", [1, 256], F32, kind="Internal").ap()

    with contextlib.ExitStack() as es:
        e.es = es
        e.s_ld = Sem(nc, es, "s_ld")
        e.g_ld = Sem(nc, es, "g_ld")
        e.s_st = Sem(nc, es, "s_st")
        e.cc = Sem(nc, es, "cc")
        e.cc_ld = Sem(nc, es, "cc_ld")
        e.store_streams = []
        e.te = Sem(nc, es, "te")
        e.ve = Sem(nc, es, "ve")
        e.se = Sem(nc, es, "se")
        e.ge = Sem(nc, es, "ge")

        e.ident_bf = es.enter_context(nc.sbuf_tensor("ident_bf", [128, 128], BF16))
        e.onesr_s = es.enter_context(nc.sbuf_tensor("onesr_s", [1, 128], F32))
        e.onecf = es.enter_context(nc.sbuf_tensor("onecf", [128, 1], F32))
        e.onecb = es.enter_context(nc.sbuf_tensor("onecb", [128, 1], BF16))
        for nm in ("s1t", "c1t", "s2t", "c2t", "s3t", "c3t", "kkt",
                   "s4t", "c4t", "s5t", "c5t"):
            setattr(e, nm, es.enter_context(nc.sbuf_tensor(nm, [128, 8], F32)))
        e.mloc = es.enter_context(nc.sbuf_tensor("mloc", [128, 1], F32))
        e.magict = es.enter_context(nc.sbuf_tensor("magict", [128, 1], F32))
        e.nmagict = es.enter_context(nc.sbuf_tensor("nmagict", [128, 1], F32))
        e.mxc = es.enter_context(nc.sbuf_tensor("mxc", [128, 32], F32))
        e.vrow = es.enter_context(nc.sbuf_tensor("vrow", [1, 128], F32))
        e.valr = es.enter_context(nc.sbuf_tensor("valr", [1, 8, 8], F32))
        e.ccs = es.enter_context(nc.sbuf_tensor("ccs", [1, 8], F32))
        e.ccr = es.enter_context(nc.sbuf_tensor("ccr", [1, 256], F32))
        e.mfin = es.enter_context(nc.sbuf_tensor("mfin", [128, 8], F32))
        e.mfin2 = es.enter_context(nc.sbuf_tensor("mfin2", [128, 8], F32))
        e.mbt = es.enter_context(nc.sbuf_tensor("mbt", [128, 8], F32))
        e.ps_misc = es.enter_context(nc.psum_tensor("ps_misc", [128, 128], F32))

        e.block = es.enter_context(nc.Block())
        emit_program(e)

    return nc



def wait_stores(e, eng):
    for st in e.store_streams:
        st.wait_all(eng)


def phase_barrier(e):
    for deco in (e.block.sync, e.block.gpsimd, e.block.vector,
                 e.block.scalar, e.block.tensor):
        @deco
        def _(eng):
            for s in (e.te, e.ve, e.se, e.ge, e.s_ld, e.g_ld, e.cc_ld):
                s.wait(eng)
            wait_stores(e, eng)


def stage_maxcols_to_cc(e, mx_ap, ncols, slot):
    @e.block.scalar
    def _(s):
        e.ve.wait(s)
        e.se.inc(s.copy(e.mbt[:], mx_ap[:, 0:8]))

    @e.block.vector
    def _(v):
        e.se.wait(v)
        e.ve.inc(v.tensor_reduce(
            e.mfin[:], mx_ap.rearrange("p (a b) -> p a b", a=8),
            AX.X, AO.max))

    @e.block.scalar
    def _(s):
        e.ve.wait(s)
        e.se.inc(s.copy(e.mfin2[:], e.mfin[:]))

    @e.block.vector
    def _(v):
        e.se.wait(v)
        e.ve.inc(v.tensor_reduce(e.mloc[:], e.mfin2[:], AX.X, AO.max))

    @e.block.scalar
    def _(s):
        e.ve.wait(s)
        e.se.inc(s.copy(e.mbt[:, 0:1], e.mloc[:]))

    @e.block.gpsimd
    def _(g):
        e.se.wait(g)
        e.cc_ld.inc(g.dma_start(
            out=e.cc_in[0:1, 0:128], in_=e.mbt[:, 0:1]), 16)


def collective_max(e, total, slot=0):
    @e.block.gpsimd
    def _(g):
        e.ve.wait(g)
        e.cc_ld.wait(g)
        e.cc.inc(g.collective_compute(
            "AllReduce", AO.max, replica_groups=[list(range(8))],
            ins=[e.cc_in[0:1, 0:128]],
            outs=[e.cc_out[0:1, slot * 128:slot * 128 + 128]]))
        e.cc.wait(g)
        e.cc_ld.inc(g.dma_start(
            out=e.ccr[0:1, slot * 128:slot * 128 + 128],
            in_=e.cc_out[0:1, slot * 128:slot * 128 + 128]), 16)


def bcast_scales(e, specs):
    for slot, width, vidx, st, ct, extra in specs:
        @e.block.vector
        def _(v, slot=slot):
            e.cc_ld.wait(v)
            e.ve.inc(v.tensor_reduce(e.valr[:, 0, 0:1],
                                     e.ccr[0:1, slot * 128: (slot + 1) * 128],
                                     AX.X, AO.max))

        @e.block.tensor
        def _(t):
            e.ve.wait(t)
            e.te.inc(t.matmul(e.ps_misc[:, 0:8], e.onesr_s[:],
                              e.valr[0:1, 0, 0:1].broadcast_to([1, 8]),
                              start=True, stop=True))

        @e.block.vector
        def _(v, st=st):
            e.te.wait(v)
            e.ve.inc(v.tensor_scalar(st[:], e.ps_misc[:, 0:8], INV32767, TINY,
                                     AO.mult, AO.max))

        @e.block.scalar
        def _(s, st=st, ct=ct, extra=extra):
            e.ve.wait(s)
            li = None
            if extra is not None:
                li = extra(s)
            if ct is not None:
                s.activation(ct[:], st[:], AF.Ln)
                li = s.activation(ct[:], ct[:], AF.Exp, scale=-1.0)
            if li is not None:
                e.se.inc(li)

        @e.block.vector
        def _(v):
            e.se.wait(v)
            e.ve.inc(v.tensor_copy(e.mbt[:], e.ps_misc[:, 0:8]))


def emit_ln(e, xn_bf):
    nc, block = e.nc, e.block
    with contextlib.ExitStack() as s1:
        x_s = s1.enter_context(nc.sbuf_tensor("x_s", [128, DC, L], F32))
        sqb = s1.enter_context(nc.sbuf_tensor("sqb", [128, DC, SL], F32))
        mu = s1.enter_context(nc.sbuf_tensor("mu", [1, SL], F32))
        e2 = s1.enter_context(nc.sbuf_tensor("e2", [1, SL], F32))
        va = s1.enter_context(nc.sbuf_tensor("va", [1, SL], F32))
        va2 = s1.enter_context(nc.sbuf_tensor("va2", [1, SL], F32))
        aco = s1.enter_context(nc.sbuf_tensor("aco", [1, SL], F32))
        bco = s1.enter_context(nc.sbuf_tensor("bco", [1, SL], F32))
        xnt = s1.enter_context(nc.sbuf_tensor("xnt", [128, SL], F32))

        ps_sx = s1.enter_context(nc.psum_tensor("ps_sx", [1, SL], F32))
        ps_sq = s1.enter_context(nc.psum_tensor("ps_sq", [1, SL], F32))
        ps_a = s1.enter_context(nc.psum_tensor("ps_a", [128, SL], F32))
        ps_b = s1.enter_context(nc.psum_tensor("ps_b", [128, SL], F32))

        @block.sync
        def _(sy):
            for dc in range(DC):
                e.s_ld.inc(sy.dma_start(
                    out=x_s[:, dc, :], in_=e.xT[dc * 128:(dc + 1) * 128, :]), 16)

        for lsl in range(NSL):
            cs = slice(lsl * SL, (lsl + 1) * SL)

            @block.scalar
            def _(s, cs=cs):
                e.s_ld.wait(s)
                e.te.wait(s)
                for dc in range(DC):
                    e.se.inc(s.activation(sqb[:, dc, :], x_s[:, dc, cs], AF.Square))

            @block.tensor
            def _(t, cs=cs):
                e.se.wait(t)
                e.ge.wait(t)
                e.ve.wait(t)
                for dc in range(DC):
                    t.matmul(ps_sx[:], e.onecf[:], x_s[:, dc, cs],
                             start=(dc == 0), stop=(dc == DC - 1))
                for dc in range(DC):
                    e.te.inc(t.matmul(ps_sq[:], e.onecf[:], sqb[:, dc, :], start=(dc == 0), stop=(dc == DC - 1)))

            @block.vector
            def _(v):
                e.te.wait(v)
                v.tensor_scalar(mu[:], ps_sx[:], 1.0 / D, None, AO.mult)
                v.tensor_scalar(e2[:], ps_sq[:], 1.0 / D, None, AO.mult)
                v.tensor_tensor(va[:], mu[:], mu[:], AO.mult)
                v.tensor_tensor(va[:], e2[:], va[:], AO.subtract)
                e.ve.inc(v.tensor_scalar(va[:], va[:], EPS_LN, None, AO.add))

            @block.scalar
            def _(s):
                e.ve.wait(s)
                s.activation(va2[:], va[:], AF.Ln)
                e.se.inc(s.activation(aco[:], va2[:], AF.Exp, scale=-0.5))

            @block.vector
            def _(v):
                e.se.wait(v)
                v.tensor_tensor(bco[:], mu[:], aco[:], AO.mult)
                e.ve.inc(v.tensor_scalar(bco[:], bco[:], -1.0, None, AO.mult))

            @block.tensor
            def _(t):
                e.ve.wait(t)
                t.matmul(ps_a[:], e.onesr_s[:], aco[:], start=True, stop=True)
                e.te.inc(t.matmul(ps_b[:], e.onesr_s[:], bco[:],
                                  start=True, stop=True))

            @block.vector
            def _(v, lsl=lsl, cs=cs):
                e.te.wait(v)
                for dc in range(DC):
                    v.tensor_tensor(xnt[:], x_s[:, dc, cs], ps_a[:], AO.mult)
                    v.tensor_tensor(xn_bf[:, dc, cs], xnt[:], ps_b[:], AO.add)
                    e.ve.inc(v.tensor_reduce(e.mxc[:, lsl * DC + dc: lsl * DC + dc + 1], xn_bf[:, dc, cs], AX.X, AO.max, apply_absolute_value=True))




def emit_xq(e, xn_bf, xq_bf):
    nc, block = e.nc, e.block
    with contextlib.ExitStack() as s0:
        yt = s0.enter_context(nc.sbuf_tensor("yt_xq", [128, SL], F32))

        @block.vector
        def _(v):
            e.ve.wait(v)
            e.se.wait(v)
            e.te.wait(v)
            for dc in range(DC):
                for lsl in range(NSL):
                    cs = slice(lsl * SL, (lsl + 1) * SL)
                    v.tensor_scalar(yt[:], xn_bf[:, dc, cs], e.c1t[:, 0:1],
                                    MAGIC, AO.mult, AO.add)
                    e.ve.inc(v.tensor_scalar(xq_bf[:, dc, cs], yt[:], MAGIC, e.s1t[:, 0:1], AO.subtract, AO.mult))


def emit_attention(e, xn_bf):
    nc, block = e.nc, e.block
    with contextlib.ExitStack() as s2:
        wbuf = s2.enter_context(nc.sbuf_tensor("wbuf", [128, DC, D], BF16))
        proj_bf = s2.enter_context(nc.sbuf_tensor("proj_bf", [128, DC, P], BF16))
        qkbuf = DBuf(nc, s2, "qkbuf", [128, DC, SL], BF16)
        qp_bf = s2.enter_context(nc.sbuf_tensor("qp_bf", [128, PC, L], BF16))
        kp_bf = s2.enter_context(nc.sbuf_tensor("kp_bf", [128, LC, P], BF16))
        kv_bf = s2.enter_context(nc.sbuf_tensor("kv_bf", [128, PC, D], BF16))
        zsum = s2.enter_context(nc.sbuf_tensor("zsum", [1, L], F32))
        zrec = zsum
        at = DBuf(nc, s2, "at_elu", [128, SL], F32)
        bt = DBuf(nc, s2, "bt_elu", [128, SL], F32)
        ps_q = s2.enter_context(nc.psum_tensor("ps_q", [128, 2, SL], F32))

        @block.gpsimd
        def _(g):
            e.te.wait(g)
            e.ve.wait(g)
            e.se.wait(g)
            for dc in range(DC):
                e.g_ld.inc(g.dma_start(out=proj_bf[:, dc, :],
                                       in_=e.proj[dc * 128:(dc + 1) * 128, :]), 16)

        for which, w_h in (("q", e.wqT), ("k", e.wkT)):
            @block.gpsimd
            def _(g, w_h=w_h):
                e.te.wait(g)
                e.ve.wait(g)
                for dc in range(DC):
                    e.g_ld.inc(g.dma_start(out=wbuf[:, dc, :],
                                           in_=w_h[dc * 128:(dc + 1) * 128, :]), 16)

            with contextlib.ExitStack() as sq:
                ps_p = sq.enter_context(nc.psum_tensor("ps_p_" + which,
                                                       [128, 4, SL], F32))
                ntile = PC if which == "q" else 4
                tw = SL if which == "q" else P
                se_w = e.se.n
                ve_w = e.ve.n
                PL = DC + ntile

                def evac_m(lsl, ec, se_w=se_w, PL=PL):
                    return se_w + lsl * PL + ec + 1

                def elu_m(tix, se_w=se_w, PL=PL, ntile=ntile):
                    return se_w + (tix // ntile) * PL + DC + tix % ntile + 1

                for lsl in range(NSL):
                    cs = slice(lsl * SL, (lsl + 1) * SL)
                    te0 = e.te.n

                    @block.tensor
                    def _(t, cs=cs, lsl=lsl, which=which):
                        e.g_ld.wait(t)
                        if which == "q" and lsl == 0:
                            e.ve.wait(t)
                            e.se.wait(t)
                        for ec in range(DC):
                            if ec >= 2:
                                t.wait_ge(e.se.h, evac_m(lsl, ec - 2))
                            for dc in range(DC):
                                mm = t.matmul(ps_q[:, ec % 2, :],
                                              wbuf[:, dc, ec * 128:(ec + 1) * 128],
                                              xn_bf[:, dc, cs],
                                              start=(dc == 0), stop=(dc == DC - 1))
                            e.te.inc(mm)

                    @block.scalar
                    def _(s, te0=te0, lsl=lsl):
                        for ec in range(DC):
                            s.wait_ge(e.te.h, te0 + ec + 1)
                            e.se.inc(s.activation(qkbuf[lsl][:, ec, :],
                                                  ps_q[:, ec % 2, :],
                                                  AF.Identity))

                    te1 = e.te.n

                    @block.tensor
                    def _(t, lsl=lsl, which=which):
                        for ti in range(ntile):
                            tix = lsl * ntile + ti
                            t.wait_ge(e.se.h, evac_m(lsl, DC - 1))
                            if tix >= 4:
                                t.wait_ge(e.se.h, elu_m(tix - 4))
                            if which == "q":
                                for ec in range(DC):
                                    mm = t.matmul(
                                        ps_p[:, tix % 4, :],
                                        proj_bf[:, ec, ti * 128:(ti + 1) * 128],
                                        qkbuf[lsl][:, ec, :],
                                        start=(ec == 0), stop=(ec == DC - 1))
                            else:
                                for ec in range(DC):
                                    mm = t.matmul(
                                        ps_p[:, tix % 4, 0:P],
                                        qkbuf[lsl][:, ec, ti * 128:(ti + 1) * 128],
                                        proj_bf[:, ec, :],
                                        start=(ec == 0), stop=(ec == DC - 1))
                            e.te.inc(mm)

                    @block.scalar
                    def _(s, lsl=lsl, te1=te1, tw=tw):
                        for ti in range(ntile):
                            tix = lsl * ntile + ti
                            s.wait_ge(e.te.h, te1 + ti + 1)
                            if tix >= 2:
                                s.wait_ge(e.ve.h, ve_w + tix - 1)
                            s.activation(at[tix][:, 0:tw], ps_p[:, tix % 4, 0:tw],
                                         AF.Relu)
                            s.activation(bt[tix][:, 0:tw], ps_p[:, tix % 4, 0:tw],
                                         AF.Relu, scale=-1.0)
                            e.se.inc(s.activation(bt[tix][:, 0:tw],
                                                  bt[tix][:, 0:tw],
                                                  AF.Exp, scale=-1.0))

                    @block.vector
                    def _(v, lsl=lsl, cs=cs, which=which, tw=tw):
                        for ti in range(ntile):
                            tix = lsl * ntile + ti
                            v.wait_ge(e.se.h, elu_m(tix))
                            if which == "q":
                                dst = qp_bf[:, ti, cs]
                            else:
                                dst = kp_bf[:, lsl * 4 + ti, :]
                            e.ve.inc(v.tensor_tensor(dst, at[tix][:, 0:tw],
                                                     bt[tix][:, 0:tw], AO.add))

        @block.gpsimd
        def _(g):
            e.te.wait(g)
            for dc in range(DC):
                e.g_ld.inc(g.dma_start(out=wbuf[:, dc, :],
                                       in_=e.wvT[dc * 128:(dc + 1) * 128, :]), 16)

        with contextlib.ExitStack() as sv:
            vbuf = DBuf(nc, sv, "vbuf", [128, SL], BF16)
            ps_kv = sv.enter_context(nc.psum_tensor("ps_kv", [128, 4, SL], F32))
            te_v = e.te.n
            se_v = e.se.n

            @block.tensor
            def _(t):
                e.g_ld.wait(t)
                e.ve.wait(t)
                for j in range(LC * 2):
                    lc, dsl = j // 2, j % 2
                    if j >= 2:
                        t.wait_ge(e.se.h, se_v + j - 1)
                    for dc in range(DC):
                        mm = t.matmul(ps_q[:, j % 2, :],
                                      xn_bf[:, dc, lc * 128:(lc + 1) * 128],
                                      wbuf[:, dc, dsl * SL:(dsl + 1) * SL],
                                      start=(dc == 0), stop=(dc == DC - 1))
                    e.te.inc(mm)
                    if j >= 1:
                        pj = j - 1
                        plc, pdsl = pj // 2, pj % 2
                        t.wait_ge(e.se.h, se_v + pj + 1)
                        for pc in range(PC):
                            t.matmul(ps_kv[:, pc * 2 + pdsl, :],
                                     kp_bf[:, plc, pc * 128:(pc + 1) * 128],
                                     vbuf[pj % 2],
                                     start=(plc == 0), stop=(plc == LC - 1))
                pj = LC * 2 - 1
                plc, pdsl = pj // 2, pj % 2
                t.wait_ge(e.se.h, se_v + pj + 1)
                for pc in range(PC):
                    mm = t.matmul(ps_kv[:, pc * 2 + pdsl, :],
                                  kp_bf[:, plc, pc * 128:(pc + 1) * 128],
                                  vbuf[pj % 2],
                                  start=(plc == 0), stop=(plc == LC - 1))
                e.te.inc(mm)

            @block.scalar
            def _(s):
                for j in range(LC * 2):
                    s.wait_ge(e.te.h, te_v + j + 1)
                    e.se.inc(s.activation(vbuf[j % 2], ps_q[:, j % 2, :],
                                          AF.Identity))
                s.wait_ge(e.te.h, te_v + LC * 2 + 1)
                for pc in range(PC):
                    for dsl in range(2):
                        e.se.inc(s.activation(
                            kv_bf[:, pc, dsl * SL:(dsl + 1) * SL],
                            ps_kv[:, pc * 2 + dsl, :], AF.Identity))

        with contextlib.ExitStack() as sz:
            ps_z = sz.enter_context(nc.psum_tensor("ps_z", [1, SL], F32))

            for lsl in range(NSL):
                cs = slice(lsl * SL, (lsl + 1) * SL)

                @block.tensor
                def _(t, cs=cs):
                    e.ve.wait(t)
                    e.se.wait(t)
                    for pc in range(PC):
                        e.te.inc(t.matmul(ps_z[:], e.onecb[:], qp_bf[:, pc, cs], start=(pc == 0), stop=(pc == PC - 1)))

                @block.vector
                def _(v, cs=cs):
                    e.te.wait(v)
                    e.ve.inc(v.tensor_scalar(zsum[0:1, cs], ps_z[:], EPS_Z,
                                             None, AO.add))

            @block.scalar
            def _(s):
                e.ve.wait(s)
                s.activation(zrec[:], zsum[:], AF.Ln)
                e.se.inc(s.activation(zrec[:], zrec[:], AF.Exp, scale=-1.0))

            @block.vector
            def _(v):
                e.se.wait(v)
                e.ve.inc(v.tensor_copy(e.vrow[0:1, 0:8], zrec[0:1, 0:8]))

        with contextlib.ExitStack() as sn:
            zbs = sn.enter_context(nc.sbuf_tensor("zbs", [128, NSL, SL], F32))
            xrl = DBuf(nc, sn, "xrl", [128, SL], F32)
            xas = DBuf(nc, sn, "xas", [128, SL], F32)
            ps_zb = sn.enter_context(nc.psum_tensor("ps_zb", [128, 2, SL], F32))
            se_z = e.se.n
            te_z = e.te.n

            @block.tensor
            def _(t):
                e.ve.wait(t)
                for lsl in range(NSL):
                    cs = slice(lsl * SL, (lsl + 1) * SL)
                    if lsl >= 2:
                        t.wait_ge(e.se.h, se_z + lsl - 1)
                    e.te.inc(t.matmul(ps_zb[:, lsl % 2, :], e.onesr_s[:],
                                      zrec[0:1, cs], start=True, stop=True))

            @block.scalar
            def _(s):
                for lsl in range(NSL):
                    s.wait_ge(e.te.h, te_z + lsl + 1)
                    e.se.inc(s.activation(zbs[:, lsl, :], ps_zb[:, lsl % 2, :],
                                          AF.Identity))

            NJ = DC * NSL
            te_base = e.te.n
            ve_base = e.ve.n
            xrl_st = Stream(e, "xrlS")
            xas_st = Stream(e, "xasS")
            e.store_streams.append(xas_st)

            @block.sync
            def _(sy, ve_base=ve_base):
                e.te.wait(sy)
                e.ve.wait(sy)
                e.se.wait(sy)
                for j in range(NJ):
                    dc, lsl = j // NSL, j % NSL
                    if j >= 2:
                        sy.wait_ge(e.ve.h, ve_base + j - 1)
                    xrl_st.inc(sy.dma_start(
                        out=xrl[j % 2],
                        in_=e.xT[dc * 128:(dc + 1) * 128,
                                 lsl * SL:(lsl + 1) * SL]), j)

            @block.tensor
            def _(t, ve_base=ve_base):
                e.se.wait(t)
                for j in range(NJ):
                    dc, lsl = j // NSL, j % NSL
                    cs = slice(lsl * SL, (lsl + 1) * SL)
                    if j >= 2:
                        t.wait_ge(e.ve.h, ve_base + j - 1)
                    for pc in range(PC):
                        mm = t.matmul(ps_q[:, j % 2, :],
                                      kv_bf[:, pc, dc * 128:(dc + 1) * 128],
                                      qp_bf[:, pc, cs],
                                      start=(pc == 0), stop=(pc == PC - 1))
                    e.te.inc(mm)

            @block.vector
            def _(v, te_base=te_base):
                for j in range(NJ):
                    dc, lsl = j // NSL, j % NSL
                    v.wait_ge(e.te.h, te_base + j + 1)
                    xrl_st.wait_done(v, j)
                    if j >= 2:
                        xas_st.wait_done(v, j - 2)
                    v.tensor_tensor(xas[j % 2], ps_q[:, j % 2, :],
                                    zbs[:, lsl, :], AO.mult)
                    e.ve.inc(v.tensor_tensor(xas[j % 2], xas[j % 2], xrl[j % 2], AO.add))

            @block.gpsimd
            def _(g, ve_base=ve_base):
                for j in range(NJ):
                    dc, lsl = j // NSL, j % NSL
                    g.wait_ge(e.ve.h, ve_base + j + 1)
                    xas_st.inc(g.dma_start(
                        out=e.xatt_h[dc][:, lsl * SL:(lsl + 1) * SL],
                        in_=xas[j % 2]), j)


class WdqLoads:

    def __init__(self, e, wn_src, ncn, wnb):
        self.e = e
        self.wn_src = wn_src
        self.ncn = ncn
        self.wnb = wnb
        self.st = Stream(e, uniq("wnbS"), nslots=wnb.n)
        self.dqsem = Sem(e.nc, e.es, uniq("dqs"))
        self.trsem = Sem(e.nc, e.es, uniq("trs"))

    def emit(self, lo, hi):
        e, wnb, wn_src = self.e, self.wnb, self.wn_src
        st, dqsem = self.st, self.dqsem

        @e.block.sync
        def _(sy):
            for ic in range(lo, hi):
                if ic >= wnb.n:
                    sy.wait_ge(dqsem.h, ic - wnb.n + 1)
                st.inc(sy.dma_start(out=wnb[ic], in_=wn_src(ic)), ic)


def emit_wdq(e, ldstate, ncn, nb, st_all, rc_all, st_act, wnb, wnat, dqt,
             ps_tr, dest, tpc, prev_se):
    nc, block = e.nc, e.block
    wnb_st, dqsem, trsem = ldstate.st, ldstate.dqsem, ldstate.trsem
    vA = e.ve.n
    seA = e.se.n
    sc_marks = []
    evac_marks = []

    def ts_mark(c):
        return vA + c + 1

    @block.scalar
    def _(s):
        def evacs(ic):
            s.wait_ge(trsem.h, trsem.n + (ic + 1) * tpc)
            e.se.inc(s.activation(
                dest(ic), ps_tr[:, ic % 2, 0:1024]
                .rearrange("p (h i) -> p h i", h=tpc), AF.Identity))
            evac_marks.append(e.se.n)

        for ic in range(ncn):
            s.wait_ge(e.ve.h, ts_mark(ic))
            if ic >= 2:
                s.wait_ge(dqsem.h, dqsem.n + ic - 1)
            s.activation(rc_all[:, ic, :], st_all[:, ic, :], AF.Ln)
            s.activation(rc_all[:, ic, :], rc_all[:, ic, :], AF.Exp,
                         scale=-1.0)
            e.se.inc(s.copy(st_act[ic], st_all[:, ic, :]))
            sc_marks.append(e.se.n)
            if ic >= 2:
                evacs(ic - 2)
        for ic in range(max(ncn - 2, 0), ncn):
            evacs(ic)

    @block.vector
    def _(v):
        v.wait_ge(e.se.h, prev_se)

        def dq(ic):
            v.wait_ge(e.se.h, sc_marks[ic])
            if ic >= 2:
                v.wait_ge(trsem.h, trsem.n + (ic - 1) * tpc)
            v.tensor_tensor(dqt[0].rearrange("p (g b) -> p g b", b=32),
                            wnb[ic].rearrange("p (g b) -> p g b", b=32),
                            rc_all[:, ic, :].rearrange("p (g o) -> p g o", o=1)
                            .broadcast_to([128, nb, 32]), AO.mult)
            v.tensor_scalar(dqt[0], dqt[0], MAGIC, MAGIC, AO.add, AO.subtract)
            dqsem.inc(v.tensor_tensor(
                wnat[ic].rearrange("p (g b) -> p g b", b=32),
                dqt[0].rearrange("p (g b) -> p g b", b=32),
                st_act[ic].rearrange("p (g o) -> p g o", o=1)
                .broadcast_to([128, nb, 32]), AO.mult))

        for ic in range(ncn):
            wnb_st.wait_done(v, ic)
            v.tensor_reduce(st_all[:, ic, :],
                            wnb[ic].rearrange("p (g b) -> p g b", b=32),
                            AX.X, AO.max, apply_absolute_value=True)
            if ic >= 1:
                e.ve.inc(v.tensor_scalar(st_all[:, ic - 1, :],
                                         st_all[:, ic - 1, :],
                                         INV7, TINY, AO.mult, AO.max))
            if ic >= 2:
                dq(ic - 2)
        e.ve.inc(v.tensor_scalar(st_all[:, ncn - 1, :],
                                 st_all[:, ncn - 1, :],
                                 INV7, TINY, AO.mult, AO.max))
        for ic in range(max(ncn - 2, 0), ncn):
            dq(ic)

    @block.tensor
    def _(t):
        for ic in range(ncn):
            for h in range(tpc):
                if h == 0:
                    t.wait_ge(dqsem.h, dqsem.n - ncn + ic + 1)
                    if ic >= 2:
                        t.wait_ge(e.se.h, evac_marks[ic - 2])
                    else:
                        t.wait_ge(e.se.h, seA)
                trsem.inc(t.transpose(ps_tr[:, ic % 2, h * 128:(h + 1) * 128],
                                      wnat[ic][:, h * 128:(h + 1) * 128],
                                      e.ident_bf[:]))


def emit_ffn_f1(e, xq_bf, wdq, spill_h):
    nc, block = e.nc, e.block
    with contextlib.ExitStack() as sf:
        stg = DBuf(nc, sf, "stg_f1", [128, L], F16)
        ps_mm = sf.enter_context(nc.psum_tensor(uniq("ps_mm_f1"), [128, 4, SL], F32))

        te_base = e.te.n
        ve_base = e.ve.n
        se_base = e.se.n
        stg_st = Stream(e, "stgS")
        e.store_streams.append(stg_st)

        @block.tensor
        def _(t):
            e.ve.wait(t)
            e.se.wait(t)
            for j in range(IC * NSL):
                ic, lsl = j // NSL, j % NSL
                cs = slice(lsl * SL, (lsl + 1) * SL)
                if j >= 4:
                    t.wait_ge(e.se.h, se_base + j - 3)
                for hc in range(DC):
                    mm = t.matmul(ps_mm[:, j % 4, :],
                                  wdq[:, hc, ic * 128:(ic + 1) * 128],
                                  xq_bf[:, hc, cs],
                                  start=(hc == 0), stop=(hc == DC - 1))
                e.te.inc(mm)

        @block.scalar
        def _(s):
            for j in range(IC * NSL):
                ic, lsl = j // NSL, j % NSL
                cs = slice(lsl * SL, (lsl + 1) * SL)
                s.wait_ge(e.te.h, te_base + j + 1)
                if lsl == 0:
                    if ic >= 2:
                        stg_st.wait_done(s, ic - 2)
                        s.wait_ge(e.ve.h, ve_base + ic - 1)
                    else:
                        wait_stores(e, s)
                e.se.inc(s.activation(stg[ic % 2][:, cs], ps_mm[:, j % 4, :],
                                      AF.Identity))

        @block.vector
        def _(v):
            for ic in range(IC):
                v.wait_ge(e.se.h, se_base + (ic + 1) * NSL)
                e.ve.inc(v.tensor_reduce(e.mxc[:, ic:ic + 1], stg[ic % 2],
                                         AX.X, AO.max,
                                         apply_absolute_value=True))

        @block.sync
        def _(sy):
            for ic in range(IC):
                sy.wait_ge(e.se.h, se_base + (ic + 1) * NSL)
                stg_st.inc(sy.dma_start(out=spill_h[ic], in_=stg[ic % 2]), ic)


def emit_sweep1(e):
    nc, block = e.nc, e.block
    with contextlib.ExitStack() as s:
        g16 = DBuf(nc, s, "g16", [128, L], F16)
        t1 = DBuf(nc, s, "sw_t1", [128, L], F32)
        t2 = DBuf(nc, s, "sw_t2", [128, L], F32)
        t3 = DBuf(nc, s, "sw_t3", [128, L], F32)
        gst = DBuf(nc, s, "gst", [128, L], F16)

        se0 = e.se.n
        ge0 = e.ge.n
        ve0 = e.ve.n
        g16_st = Stream(e, "g16S")
        gst_st = Stream(e, "gstS")
        pre1_marks = []
        tsd_marks = []
        red_marks = []

        def act_mark(ic):
            return se0 + ic + 1

        def tt_mark(p):
            return ge0 + p + 1

        def pre1_arith(ic):
            return ve0 + (ic + 1 if ic < 2 else 3 * ic - 2)

        @block.sync
        def _(sy):
            wait_stores(e, sy)
            for ic in range(IC):
                if ic >= 2:
                    sy.wait_ge(e.ve.h, pre1_arith(ic - 2))
                g16_st.inc(sy.dma_start(out=g16[ic], in_=e.gate_h[ic]), ic)

        e.store_streams.append(gst_st)

        @block.vector
        def _(v):
            wait_stores(e, v)

            for ic in range(IC):
                g16_st.wait_done(v, ic)
                if ic >= 2:
                    v.wait_ge(e.ge.h, tt_mark(ic - 2))
                e.ve.inc(v.tensor_scalar(t1[ic], g16[ic], e.c3t[:, 0:1],
                                         MAGIC, AO.mult, AO.add))
                pre1_marks.append(e.ve.n)
                if ic >= 1:
                    p = ic - 1
                    v.wait_ge(e.se.h, act_mark(p))
                    e.ve.inc(v.tensor_scalar(t1[p], t3[p], MAGIC65535, MAGIC,
                                             AO.min, AO.subtract))
                    tsd_marks.append(e.ve.n)
                if ic >= 2:
                    p = ic - 2
                    v.wait_ge(e.ge.h, tt_mark(p))
                    e.ve.inc(v.tensor_reduce(e.mxc[:, p:p + 1], gst[p],
                                             AX.X, AO.max,
                                             apply_absolute_value=True))
                    red_marks.append(e.ve.n)
            p = IC - 1
            v.wait_ge(e.se.h, act_mark(p))
            e.ve.inc(v.tensor_scalar(t1[p], t3[p], MAGIC65535, MAGIC,
                                     AO.min, AO.subtract))
            tsd_marks.append(e.ve.n)
            for p in (IC - 2, IC - 1):
                v.wait_ge(e.ge.h, tt_mark(p))
                e.ve.inc(v.tensor_reduce(e.mxc[:, p:p + 1], gst[p],
                                         AX.X, AO.max,
                                         apply_absolute_value=True))
                red_marks.append(e.ve.n)

        @block.scalar
        def _(s_):
            for ic in range(IC):
                s_.wait_ge(e.ve.h, pre1_marks[ic])
                if ic >= 2:
                    s_.wait_ge(e.ve.h, tsd_marks[ic - 2])
                    s_.wait_ge(e.ge.h, tt_mark(ic - 2))
                s_.activation(t2[ic], t1[ic], AF.Identity,
                              bias=e.nmagict[:, 0:1])
                s_.activation(t2[ic], t2[ic], AF.Identity,
                              scale=e.kkt[:, 0:1])
                s_.activation(t3[ic], t2[ic], AF.Sigmoid, scale=65536.0)
                e.se.inc(s_.activation(t3[ic], t3[ic], AF.Identity,
                                       bias=e.magict[:, 0:1], scale=65536.0))

        @block.gpsimd
        def _(g):
            for p in range(IC):
                g.wait_ge(e.ve.h, tsd_marks[p])
                if p >= 2:
                    g.wait_ge(e.ve.h, red_marks[p - 2])
                    gst_st.wait_done(g, p - 2)
                e.ge.inc(g.tensor_tensor(gst[p], t2[p], t1[p], AO.mult))
                gst_st.inc(g.dma_start(out=e.gs_h[p], in_=gst[p]), p)


def emit_sweep2(e):
    nc, block = e.nc, e.block
    with contextlib.ExitStack() as s:
        u16 = DBuf(nc, s, "u16", [128, L], F16)
        g16 = DBuf(nc, s, "gs16", [128, L], F16)
        t1 = DBuf(nc, s, "s2_t1", [128, L], F32)
        t2 = DBuf(nc, s, "s2_t2", [128, L], F32)
        ost = DBuf(nc, s, "ost", [128, L], F16)

        se0 = e.se.n
        ge0 = e.ge.n
        ve0 = e.ve.n
        u16_st = Stream(e, "u16S")
        gs16_st = Stream(e, "gs16S")
        ost_st = Stream(e, "ostS")
        gs_marks = []
        red_marks = []

        def act_mark(ic):
            return se0 + ic + 1

        def tt_mark(p):
            return ge0 + p + 1

        def gs_arith(ic):
            return ve0 + (ic + 1 if ic < 2 else 2 * ic - 1)

        @block.sync
        def _(sy):
            wait_stores(e, sy)
            for ic in range(IC):
                if ic >= 2:
                    sy.wait_ge(e.se.h, act_mark(ic - 2))
                    sy.wait_ge(e.ve.h, gs_arith(ic - 2))
                u16_st.inc(sy.dma_start(out=u16[ic], in_=e.up_h[ic]), ic)
                gs16_st.inc(sy.dma_start(out=g16[ic], in_=e.gs_h[ic]), ic)

        e.store_streams.append(ost_st)

        @block.vector
        def _(v):
            wait_stores(e, v)
            for ic in range(IC):
                gs16_st.wait_done(v, ic)
                if ic >= 2:
                    v.wait_ge(e.ge.h, tt_mark(ic - 2))
                v.tensor_scalar(t2[ic], g16[ic], e.c4t[:, 0:1], MAGIC,
                                AO.mult, AO.add)
                e.ve.inc(v.tensor_scalar(t2[ic], t2[ic], MAGIC, e.s4t[:, 0:1],
                                         AO.subtract, AO.mult))
                gs_marks.append(e.ve.n)
                if ic >= 2:
                    p = ic - 2
                    v.wait_ge(e.ge.h, tt_mark(p))
                    e.ve.inc(v.tensor_reduce(e.mxc[:, p:p + 1], ost[p],
                                             AX.X, AO.max,
                                             apply_absolute_value=True))
                    red_marks.append(e.ve.n)
            for p in (IC - 2, IC - 1):
                v.wait_ge(e.ge.h, tt_mark(p))
                e.ve.inc(v.tensor_reduce(e.mxc[:, p:p + 1], ost[p],
                                         AX.X, AO.max,
                                         apply_absolute_value=True))
                red_marks.append(e.ve.n)

        @block.scalar
        def _(s_):
            for ic in range(IC):
                u16_st.wait_done(s_, ic)
                if ic >= 2:
                    s_.wait_ge(e.ge.h, tt_mark(ic - 2))
                s_.activation(t1[ic], u16[ic], AF.Identity,
                              scale=e.c2t[:, 0:1], bias=e.magict[:, 0:1])
                s_.activation(t1[ic], t1[ic], AF.Identity,
                              bias=e.nmagict[:, 0:1])
                e.se.inc(s_.activation(t1[ic], t1[ic], AF.Identity,
                                       scale=e.s2t[:, 0:1]))

        @block.gpsimd
        def _(g):
            for p in range(IC):
                g.wait_ge(e.ve.h, gs_marks[p])
                g.wait_ge(e.se.h, act_mark(p))
                if p >= 2:
                    g.wait_ge(e.ve.h, red_marks[p - 2])
                    ost_st.wait_done(g, p - 2)
                e.ge.inc(g.tensor_tensor(ost[p], t1[p], t2[p], AO.mult))
                ost_st.inc(g.dma_start(out=e.gs_h[p], in_=ost[p]), p)


def emit_sweep3(e, wdq_dn):
    nc, block = e.nc, e.block
    with contextlib.ExitStack() as s:
        ob16 = DBuf(nc, s, "ob16", [128, SL], F16)
        obf = DBuf(nc, s, "obf", [128, IC, SL], BF16)
        yt = DBuf(nc, s, "yt_s3", [128, SL], F32)
        xrl2 = DBuf(nc, s, "xrl2", [128, SL], F32)
        ostg = DBuf(nc, s, "ostg", [128, SL], F32)
        ps_dn = s.enter_context(nc.psum_tensor("ps_dn", [128, 4, SL], F32))
        ob16_st = Stream(e, "ob16S")
        xrl2_st = Stream(e, "xrl2S")
        ostg_st = Stream(e, "ostgS")
        e.store_streams.append(ostg_st)

        se0 = e.se.n
        te0 = e.te.n
        conv_marks = {}
        ostg_marks = {}

        def act_mark(j):
            return se0 + j + 1

        def mm_mark(lsl, hc):
            return te0 + lsl * DC + hc + 1

        n = e.ve.n
        for lsl in range(NSL):
            for ic in range(IC):
                n += 1
                conv_marks[(lsl, ic)] = n
            if lsl >= 1:
                for hc in range(DC):
                    n += 1
                    ostg_marks[(lsl - 1, hc)] = n
        for hc in range(DC):
            n += 1
            ostg_marks[(NSL - 1, hc)] = n

        @block.scalar
        def _(s_):
            for lsl in range(NSL):
                for ic in range(IC):
                    j = lsl * IC + ic
                    ob16_st.wait_done(s_, j)
                    if j >= 2:
                        s_.wait_ge(e.ve.h, conv_marks[divmod(j - 2, IC)])
                    s_.activation(yt[j], ob16[j], AF.Identity,
                                  scale=e.c5t[:, 0:1], bias=e.magict[:, 0:1])
                    e.se.inc(s_.activation(yt[j], yt[j], AF.Identity,
                                           bias=e.nmagict[:, 0:1]))

        @block.vector
        def _(v):
            wait_stores(e, v)

            def convs(lsl):
                for ic in range(IC):
                    j = lsl * IC + ic
                    v.wait_ge(e.se.h, act_mark(j))
                    if lsl >= 2 and ic == 0:
                        v.wait_ge(e.te.h, mm_mark(lsl - 2, DC - 1))
                    e.ve.inc(v.tensor_scalar(obf[lsl][:, ic, :], yt[j],
                                             e.s5t[:, 0:1], None, AO.mult))

            def ostg_row(lsl):
                for hc in range(DC):
                    g = lsl * DC + hc
                    v.wait_ge(e.te.h, mm_mark(lsl, hc))
                    xrl2_st.wait_done(v, g)
                    if g >= 2:
                        ostg_st.wait_done(v, g - 2)
                    e.ve.inc(v.tensor_tensor(ostg[g], ps_dn[:, g % 4, :],
                                             xrl2[g], AO.add))

            for lsl in range(NSL):
                convs(lsl)
                if lsl >= 1:
                    ostg_row(lsl - 1)
            ostg_row(NSL - 1)

        @block.sync
        def _(sy):
            wait_stores(e, sy)
            for lsl in range(NSL):
                for ic in range(IC):
                    j = lsl * IC + ic
                    if j >= 2:
                        sy.wait_ge(e.se.h, act_mark(j - 2))
                    ob16_st.inc(sy.dma_start(
                        out=ob16[j],
                        in_=e.gs_h[ic][:, lsl * SL:(lsl + 1) * SL]), j)

        @block.tensor
        def _(t):
            t.wait_ge(e.se.h, se0)
            for lsl in range(NSL):
                cs = slice(lsl * SL, (lsl + 1) * SL)
                for hc in range(DC):
                    g = lsl * DC + hc
                    if hc == 0:
                        t.wait_ge(e.ve.h, conv_marks[(lsl, IC - 1)])
                    if g >= 4:
                        t.wait_ge(e.ve.h, ostg_marks[divmod(g - 4, DC)])
                    for ic in range(IC):
                        mm = t.matmul(ps_dn[:, g % 4, :],
                                      wdq_dn[:, ic, hc * 128:(hc + 1) * 128],
                                      obf[lsl][:, ic, :],
                                      start=(ic == 0), stop=(ic == IC - 1))
                    e.te.inc(mm)

        NJ = NSL * DC

        def store_out(g_, g):
            lsl, hc = divmod(g, DC)
            g_.wait_ge(e.ve.h, ostg_marks[(lsl, hc)])
            ostg_st.inc(g_.dma_start(
                out=e.out[hc * 128:(hc + 1) * 128,
                          lsl * SL:(lsl + 1) * SL],
                in_=ostg[g]), g)

        @block.gpsimd
        def _(g_):
            wait_stores(e, g_)
            for g in range(NJ):
                if g >= 2:
                    store_out(g_, g - 2)
                lsl, hc = divmod(g, DC)
                xrl2_st.inc(g_.dma_start(
                    out=xrl2[g],
                    in_=e.xatt_h[hc][:, lsl * SL:(lsl + 1) * SL]), g)
            store_out(g_, NJ - 2)
            store_out(g_, NJ - 1)



def emit_program(e):
    nc, block = e.nc, e.block

    @block.sync
    def _(sy):
        e.s_ld.inc(sy.dma_start(out=e.onesr_s[:], in_=e.onesr[:]), 16)

    @block.tensor
    def _(t):
        e.s_ld.wait(t)
        e.te.inc(t.matmul(e.ps_misc[:, 0:128], e.onesr_s[:], e.onesr_s[:],
                          start=True, stop=True))

    @block.gpsimd
    def _(g):
        e.g_ld.inc(g.dma_start(out=e.ident_bf[:], in_=e.ident[:]), 16)
        e.ge.inc(g.memset(e.onecf[:], 1.0))
        e.ge.inc(g.memset(e.onecb[:], 1.0))
        e.ge.inc(g.memset(e.mloc[:], 0.0))
        e.ge.inc(g.memset(e.magict[:], MAGIC))
        e.ge.inc(g.memset(e.nmagict[:], -MAGIC))

    def set_kk(s):
        return s.activation(e.kkt[:], e.s3t[:], AF.Identity, scale=K216)

    with contextlib.ExitStack() as sc:
        wdq = sc.enter_context(nc.sbuf_tensor("wdq", [128, DC * I], BF16))
        wdq_ud = wdq[:].rearrange("p (h i) -> p h i", h=DC)
        wdq_dn = wdq[:].rearrange("p (i h) -> p i h", i=IC)
        wnb = DBuf(nc, sc, "wnb", [128, D], F32, nslots=4)
        st_all = sc.enter_context(nc.sbuf_tensor("st_all", [128, IC, 32], F32))
        rc_all = sc.enter_context(nc.sbuf_tensor("rc_all", [128, IC, 32], F32))
        st_act = DBuf(nc, sc, "st_act", [128, 32], F32)
        wnat = DBuf(nc, sc, "wnat", [128, D], BF16)
        dqt = DBuf(nc, sc, "dqt", [128, D], F32, nslots=1)
        dest_ud = lambda ic: wdq_ud[:, 0:DC, ic * 128:(ic + 1) * 128]

        xn_bf = sc.enter_context(nc.sbuf_tensor("xn_bf", [128, DC, L], BF16))
        emit_ln(e, xn_bf)
        ld_gate = WdqLoads(
            e, lambda ic: e.wgate_n[ic * 128:(ic + 1) * 128, :], IC, wnb)
        ld_gate.emit(0, wnb.n)
        emit_attention(e, xn_bf)
        ld_gate.emit(wnb.n, IC)

        stage_maxcols_to_cc(e, e.mxc[:, 0:32], 32, 0)
        collective_max(e, 1024, 0)

        ps_tr = sc.enter_context(nc.psum_tensor("ps_tr", [128, 2, 1024],
                                                BF16))
        prev_se = e.se.n
        emit_wdq(e, ld_gate, IC, 32, st_all, rc_all, st_act, wnb, wnat,
                 dqt, ps_tr, dest_ud, DC, prev_se)

        bcast_scales(e, [(0, 1024, 0, e.s1t, e.c1t, None)])
        xq_bf = xn_bf
        emit_xq(e, xn_bf, xq_bf)

        emit_ffn_f1(e, xq_bf, wdq_ud, e.gate_h)
        stage_maxcols_to_cc(e, e.mxc[:, 0:IC], IC, 1)
        collective_max(e, 1024, 1)
        ld_up = WdqLoads(
            e, lambda ic: e.wup_n[ic * 128:(ic + 1) * 128, :], IC, wnb)
        ld_up.emit(0, IC)
        prev_se = e.se.n
        emit_wdq(e, ld_up, IC, 32, st_all, rc_all, st_act, wnb, wnat,
                 dqt, ps_tr, dest_ud, DC, prev_se)
        bcast_scales(e, [(1, 1024, 1, e.s3t, e.c3t, set_kk)])

        emit_ffn_f1(e, xq_bf, wdq_ud, e.up_h)
        stage_maxcols_to_cc(e, e.mxc[:, 0:IC], IC, 0)
        collective_max(e, 1024, 0)

        emit_sweep1(e)
        ld_dn = WdqLoads(
            e, lambda ic: e.wdown_n[(ic // 3) * 128:(ic // 3 + 1) * 128,
                                    (ic % 3) * 1024:(ic % 3 + 1) * 1024],
            IC, wnb)
        ld_dn.emit(0, IC)
        stage_maxcols_to_cc(e, e.mxc[:, 0:IC], IC, 1)
        collective_max(e, 1024, 1)
        bcast_scales(e, [(0, 1024, 0, e.s2t, e.c2t, None)])

        prev_se = e.se.n
        emit_wdq(e, ld_dn, IC, 32, st_all, rc_all, st_act, wnb, wnat,
                 dqt, ps_tr,
                 lambda ic: wdq_dn[:, (ic % 3) * 8:(ic % 3) * 8 + 8,
                                   (ic // 3) * 128:(ic // 3 + 1) * 128],
                 DC, prev_se)
        bcast_scales(e, [(1, 1024, 1, e.s4t, e.c4t, None)])

        emit_sweep2(e)
        stage_maxcols_to_cc(e, e.mxc[:, 0:IC], IC, 0)
        collective_max(e, 1024, 0)
        bcast_scales(e, [(0, 1024, 0, e.s5t, e.c5t, None)])

        emit_sweep3(e, wdq_dn)

    @block.sync
    def _(sy):
        wait_stores(e, sy)



def kernel(L0_attn_x, L0_attn_Wq, L0_attn_Wk, L0_attn_Wv, L0_attn_proj,
           L0_ffn_w_gate, L0_ffn_w_up, L0_ffn_w_down):
    X = np.ascontiguousarray(np.asarray(L0_attn_x, dtype=np.float32))
    Wq = np.asarray(L0_attn_Wq, dtype=np.float32)
    Wk = np.asarray(L0_attn_Wk, dtype=np.float32)
    Wv = np.asarray(L0_attn_Wv, dtype=np.float32)
    proj = np.ascontiguousarray(np.asarray(L0_attn_proj, dtype=np.float32))
    w_gate = np.ascontiguousarray(np.asarray(L0_ffn_w_gate, dtype=np.float32))
    w_up = np.ascontiguousarray(np.asarray(L0_ffn_w_up, dtype=np.float32))
    w_down = np.ascontiguousarray(np.asarray(L0_ffn_w_down, dtype=np.float32))

    wqT = np.ascontiguousarray(Wq.T)
    wkT = np.ascontiguousarray(Wk.T)
    wvT = np.ascontiguousarray(Wv.T)

    ident = np.eye(128, dtype=np.float32)
    onesr = np.ones((1, 128), dtype=np.float32)

    nc = build_nc()
    in_maps = []
    for b in range(B):
        in_maps.append({
            "xT": np.ascontiguousarray(X[b].T),
            "wqT": wqT, "wkT": wkT, "wvT": wvT, "proj": proj,
            "wup_n": w_up, "wgate_n": w_gate, "wdown_n": w_down,
            "ident": ident, "onesr": onesr,
        })
    res = run_bass_kernel_spmd(nc, in_maps, core_ids=list(range(B)))
    out = np.stack([np.asarray(res.results[b]["out"]).T for b in range(B)], axis=0)
    return np.ascontiguousarray(out, dtype=np.float32)


# version 15
# speedup vs baseline: 1.1121x; 1.0092x over previous
import contextlib
import numpy as np
import concourse.bass as bass
import concourse.mybir as mybir
from concourse.bass_utils import run_bass_kernel_spmd

F32 = mybir.dt.float32
BF16 = mybir.dt.bfloat16
F16 = mybir.dt.float16
AO = mybir.AluOpType
AF = mybir.ActivationFunctionType
AX = mybir.AxisListType

MAGIC = 12582912.0
MAGIC65535 = MAGIC + 65535.0
INV7 = 1.0 / 7.0
INV32767 = 1.0 / 32767.0
TINY = 1e-12
EPS_LN = 1e-5
EPS_Z = 1e-8
K216 = 2.0 ** -16

B, L, D, P, I = 8, 2048, 1024, 256, 3072
DC, IC, PC, LC = 8, 24, 2, 16
NSL, SL = 4, 512


class Sem:
    def __init__(self, nc, es, name):
        self.h = es.enter_context(nc.semaphore(name))
        self.n = 0

    def inc(self, instr, k=1):
        instr.then_inc(self.h, k)
        self.n += k
        return instr

    def wait(self, eng, val=None):
        eng.wait_ge(self.h, self.n if val is None else val)


class Env:
    pass


_UNIQ = [0]


def uniq(name):
    _UNIQ[0] += 1
    return f"{name}_u{_UNIQ[0]}"


class Stream:

    def __init__(self, e, name, nslots=2):
        self.sems = [Sem(e.nc, e.es, uniq(f"{name}{i}")) for i in range(nslots)]
        self.nslots = nslots

    def inc(self, instr, j):
        self.sems[j % self.nslots].inc(instr, 16)

    def wait_done(self, eng, j):
        s = self.sems[j % self.nslots]
        eng.wait_ge(s.h, 16 * ((j // self.nslots) + 1))

    def wait_all(self, eng):
        for s in self.sems:
            if s.n:
                eng.wait_ge(s.h, s.n)


class DBuf:
    def __init__(self, nc, es, name, shape, dtype, nslots=2):
        self.t = [es.enter_context(nc.sbuf_tensor(uniq(f"{name}_{i}"), shape, dtype))
                  for i in range(nslots)]
        self.n = nslots

    def __getitem__(self, i):
        return self.t[i % self.n][:]



def build_nc():
    nc = bass.Bass("TRN2", target_bir_lowering=False, debug=False,
                   detect_race_conditions=False)
    e = Env()
    e.nc = nc

    def din(name, shape, dtype=F32):
        return nc.dram_tensor(name, shape, dtype, kind="ExternalInput").ap()

    e.xT = din("xT", [D, L])
    e.wqT = din("wqT", [D, D])
    e.wkT = din("wkT", [D, D])
    e.wvT = din("wvT", [D, D])
    e.proj = din("proj", [D, P])
    e.wup_n = din("wup_n", [I, D])
    e.wgate_n = din("wgate_n", [I, D])
    e.wdown_n = din("wdown_n", [D, I])
    e.ident = din("ident", [128, 128])
    e.onesr = din("onesr", [1, 128])

    e.out = nc.dram_tensor("out", [D, L], F32, kind="ExternalOutput").ap()

    e.xatt_h = nc.dram_tensor("xatt_sp", [DC, 128, L], F32, kind="Internal").ap()
    e.gate_h = nc.dram_tensor("gate_sp", [IC, 128, L], F16, kind="Internal").ap()
    e.up_h = nc.dram_tensor("up_sp", [IC, 128, L], F16, kind="Internal").ap()
    e.gs_h = nc.dram_tensor("gs_sp", [IC, 128, L], F16, kind="Internal").ap()
    e.cc_in = nc.dram_tensor("cc_in", [1, 128], F32, kind="Internal").ap()
    e.cc_out = nc.dram_tensor("cc_out", [1, 256], F32, kind="Internal").ap()

    with contextlib.ExitStack() as es:
        e.es = es
        e.s_ld = Sem(nc, es, "s_ld")
        e.g_ld = Sem(nc, es, "g_ld")
        e.s_st = Sem(nc, es, "s_st")
        e.cc = Sem(nc, es, "cc")
        e.cc_ld = Sem(nc, es, "cc_ld")
        e.store_streams = []
        e.te = Sem(nc, es, "te")
        e.ve = Sem(nc, es, "ve")
        e.se = Sem(nc, es, "se")
        e.ge = Sem(nc, es, "ge")

        e.ident_bf = es.enter_context(nc.sbuf_tensor("ident_bf", [128, 128], BF16))
        e.onesr_s = es.enter_context(nc.sbuf_tensor("onesr_s", [1, 128], F32))
        e.onecf = es.enter_context(nc.sbuf_tensor("onecf", [128, 1], F32))
        e.onecb = es.enter_context(nc.sbuf_tensor("onecb", [128, 1], BF16))
        for nm in ("s1t", "c1t", "s2t", "c2t", "s3t", "c3t", "kkt",
                   "s4t", "c4t", "s5t", "c5t"):
            setattr(e, nm, es.enter_context(nc.sbuf_tensor(nm, [128, 8], F32)))
        e.mloc = es.enter_context(nc.sbuf_tensor("mloc", [128, 1], F32))
        e.magict = es.enter_context(nc.sbuf_tensor("magict", [128, 1], F32))
        e.nmagict = es.enter_context(nc.sbuf_tensor("nmagict", [128, 1], F32))
        e.mxc = es.enter_context(nc.sbuf_tensor("mxc", [128, 32], F32))
        e.vrow = es.enter_context(nc.sbuf_tensor("vrow", [1, 128], F32))
        e.valr = es.enter_context(nc.sbuf_tensor("valr", [1, 8, 8], F32))
        e.ccs = es.enter_context(nc.sbuf_tensor("ccs", [1, 8], F32))
        e.ccr = es.enter_context(nc.sbuf_tensor("ccr", [1, 256], F32))
        e.mfin = es.enter_context(nc.sbuf_tensor("mfin", [128, 8], F32))
        e.mfin2 = es.enter_context(nc.sbuf_tensor("mfin2", [128, 8], F32))
        e.mbt = es.enter_context(nc.sbuf_tensor("mbt", [128, 8], F32))
        e.ps_misc = es.enter_context(nc.psum_tensor("ps_misc", [128, 128], F32))

        e.block = es.enter_context(nc.Block())
        emit_program(e)

    return nc



def wait_stores(e, eng):
    for st in e.store_streams:
        st.wait_all(eng)


def phase_barrier(e):
    for deco in (e.block.sync, e.block.gpsimd, e.block.vector,
                 e.block.scalar, e.block.tensor):
        @deco
        def _(eng):
            for s in (e.te, e.ve, e.se, e.ge, e.s_ld, e.g_ld, e.cc_ld):
                s.wait(eng)
            wait_stores(e, eng)


def stage_maxcols_to_cc(e, mx_ap, ncols, slot):
    @e.block.scalar
    def _(s):
        e.ve.wait(s)
        e.se.inc(s.copy(e.mbt[:], mx_ap[:, 0:8]))

    @e.block.vector
    def _(v):
        e.se.wait(v)
        e.ve.inc(v.tensor_reduce(
            e.mfin[:], mx_ap.rearrange("p (a b) -> p a b", a=8),
            AX.X, AO.max))

    @e.block.scalar
    def _(s):
        e.ve.wait(s)
        e.se.inc(s.copy(e.mfin2[:], e.mfin[:]))

    @e.block.vector
    def _(v):
        e.se.wait(v)
        e.ve.inc(v.tensor_reduce(e.mloc[:], e.mfin2[:], AX.X, AO.max))

    @e.block.scalar
    def _(s):
        e.ve.wait(s)
        e.se.inc(s.copy(e.mbt[:, 0:1], e.mloc[:]))

    @e.block.gpsimd
    def _(g):
        e.se.wait(g)
        e.cc_ld.inc(g.dma_start(
            out=e.cc_in[0:1, 0:128], in_=e.mbt[:, 0:1]), 16)


def collective_max(e, total, slot=0):
    @e.block.gpsimd
    def _(g):
        e.ve.wait(g)
        e.cc_ld.wait(g)
        e.cc.inc(g.collective_compute(
            "AllReduce", AO.max, replica_groups=[list(range(8))],
            ins=[e.cc_in[0:1, 0:128]],
            outs=[e.cc_out[0:1, slot * 128:slot * 128 + 128]]))
        e.cc.wait(g)
        e.cc_ld.inc(g.dma_start(
            out=e.ccr[0:1, slot * 128:slot * 128 + 128],
            in_=e.cc_out[0:1, slot * 128:slot * 128 + 128]), 16)


def bcast_scales(e, specs):
    for slot, width, vidx, st, ct, extra in specs:
        @e.block.vector
        def _(v, slot=slot):
            e.cc_ld.wait(v)
            e.ve.inc(v.tensor_reduce(e.valr[:, 0, 0:1],
                                     e.ccr[0:1, slot * 128: (slot + 1) * 128],
                                     AX.X, AO.max))

        @e.block.tensor
        def _(t):
            e.ve.wait(t)
            e.te.inc(t.matmul(e.ps_misc[:, 0:8], e.onesr_s[:],
                              e.valr[0:1, 0, 0:1].broadcast_to([1, 8]),
                              start=True, stop=True))

        @e.block.vector
        def _(v, st=st):
            e.te.wait(v)
            e.ve.inc(v.tensor_scalar(st[:], e.ps_misc[:, 0:8], INV32767, TINY,
                                     AO.mult, AO.max))

        @e.block.scalar
        def _(s, st=st, ct=ct, extra=extra):
            e.ve.wait(s)
            li = None
            if extra is not None:
                li = extra(s)
            if ct is not None:
                s.activation(ct[:], st[:], AF.Ln)
                li = s.activation(ct[:], ct[:], AF.Exp, scale=-1.0)
            if li is not None:
                e.se.inc(li)

        @e.block.vector
        def _(v):
            e.se.wait(v)
            e.ve.inc(v.tensor_copy(e.mbt[:], e.ps_misc[:, 0:8]))


def emit_ln(e, xn_bf):
    nc, block = e.nc, e.block
    with contextlib.ExitStack() as s1:
        x_s = s1.enter_context(nc.sbuf_tensor("x_s", [128, DC, L], F32))
        sqb = s1.enter_context(nc.sbuf_tensor("sqb", [128, DC, SL], F32))
        mu = s1.enter_context(nc.sbuf_tensor("mu", [1, SL], F32))
        e2 = s1.enter_context(nc.sbuf_tensor("e2", [1, SL], F32))
        va = s1.enter_context(nc.sbuf_tensor("va", [1, SL], F32))
        va2 = s1.enter_context(nc.sbuf_tensor("va2", [1, SL], F32))
        aco = s1.enter_context(nc.sbuf_tensor("aco", [1, SL], F32))
        bco = s1.enter_context(nc.sbuf_tensor("bco", [1, SL], F32))
        xnt = s1.enter_context(nc.sbuf_tensor("xnt", [128, SL], F32))

        ps_sx = s1.enter_context(nc.psum_tensor("ps_sx", [1, SL], F32))
        ps_sq = s1.enter_context(nc.psum_tensor("ps_sq", [1, SL], F32))
        ps_a = s1.enter_context(nc.psum_tensor("ps_a", [128, SL], F32))
        ps_b = s1.enter_context(nc.psum_tensor("ps_b", [128, SL], F32))

        @block.sync
        def _(sy):
            for dc in range(DC):
                e.s_ld.inc(sy.dma_start(
                    out=x_s[:, dc, :], in_=e.xT[dc * 128:(dc + 1) * 128, :]), 16)

        for lsl in range(NSL):
            cs = slice(lsl * SL, (lsl + 1) * SL)

            @block.scalar
            def _(s, cs=cs):
                e.s_ld.wait(s)
                e.te.wait(s)
                for dc in range(DC):
                    e.se.inc(s.activation(sqb[:, dc, :], x_s[:, dc, cs], AF.Square))

            @block.tensor
            def _(t, cs=cs):
                e.se.wait(t)
                e.ge.wait(t)
                e.ve.wait(t)
                for dc in range(DC):
                    t.matmul(ps_sx[:], e.onecf[:], x_s[:, dc, cs],
                             start=(dc == 0), stop=(dc == DC - 1))
                for dc in range(DC):
                    e.te.inc(t.matmul(ps_sq[:], e.onecf[:], sqb[:, dc, :], start=(dc == 0), stop=(dc == DC - 1)))

            @block.vector
            def _(v):
                e.te.wait(v)
                v.tensor_scalar(mu[:], ps_sx[:], 1.0 / D, None, AO.mult)
                v.tensor_scalar(e2[:], ps_sq[:], 1.0 / D, None, AO.mult)
                v.tensor_tensor(va[:], mu[:], mu[:], AO.mult)
                v.tensor_tensor(va[:], e2[:], va[:], AO.subtract)
                e.ve.inc(v.tensor_scalar(va[:], va[:], EPS_LN, None, AO.add))

            @block.scalar
            def _(s):
                e.ve.wait(s)
                s.activation(va2[:], va[:], AF.Ln)
                e.se.inc(s.activation(aco[:], va2[:], AF.Exp, scale=-0.5))

            @block.vector
            def _(v):
                e.se.wait(v)
                v.tensor_tensor(bco[:], mu[:], aco[:], AO.mult)
                e.ve.inc(v.tensor_scalar(bco[:], bco[:], -1.0, None, AO.mult))

            @block.tensor
            def _(t):
                e.ve.wait(t)
                t.matmul(ps_a[:], e.onesr_s[:], aco[:], start=True, stop=True)
                e.te.inc(t.matmul(ps_b[:], e.onesr_s[:], bco[:],
                                  start=True, stop=True))

            @block.vector
            def _(v, lsl=lsl, cs=cs):
                e.te.wait(v)
                for dc in range(DC):
                    v.tensor_tensor(xnt[:], x_s[:, dc, cs], ps_a[:], AO.mult)
                    v.tensor_tensor(xn_bf[:, dc, cs], xnt[:], ps_b[:], AO.add)
                    e.ve.inc(v.tensor_reduce(e.mxc[:, lsl * DC + dc: lsl * DC + dc + 1], xn_bf[:, dc, cs], AX.X, AO.max, apply_absolute_value=True))




def emit_xq(e, xn_bf, xq_bf):
    nc, block = e.nc, e.block
    with contextlib.ExitStack() as s0:
        yt = s0.enter_context(nc.sbuf_tensor("yt_xq", [128, SL], F32))

        @block.vector
        def _(v):
            e.ve.wait(v)
            e.se.wait(v)
            e.te.wait(v)
            for dc in range(DC):
                for lsl in range(NSL):
                    cs = slice(lsl * SL, (lsl + 1) * SL)
                    v.tensor_scalar(yt[:], xn_bf[:, dc, cs], e.c1t[:, 0:1],
                                    MAGIC, AO.mult, AO.add)
                    e.ve.inc(v.tensor_scalar(xq_bf[:, dc, cs], yt[:], MAGIC, e.s1t[:, 0:1], AO.subtract, AO.mult))


def emit_attention(e, xn_bf):
    nc, block = e.nc, e.block
    with contextlib.ExitStack() as s2:
        wbuf = s2.enter_context(nc.sbuf_tensor("wbuf", [128, DC, D], BF16))
        proj_bf = s2.enter_context(nc.sbuf_tensor("proj_bf", [128, DC, P], BF16))
        qkbuf = DBuf(nc, s2, "qkbuf", [128, DC, SL], BF16)
        qp_bf = s2.enter_context(nc.sbuf_tensor("qp_bf", [128, PC, L], BF16))
        kp_bf = s2.enter_context(nc.sbuf_tensor("kp_bf", [128, LC, P], BF16))
        kv_bf = s2.enter_context(nc.sbuf_tensor("kv_bf", [128, PC, D], BF16))
        zsum = s2.enter_context(nc.sbuf_tensor("zsum", [1, L], F32))
        zrec = zsum
        at = DBuf(nc, s2, "at_elu", [128, SL], F32)
        bt = DBuf(nc, s2, "bt_elu", [128, SL], F32)
        ps_q = s2.enter_context(nc.psum_tensor("ps_q", [128, 2, SL], F32))

        @block.gpsimd
        def _(g):
            e.te.wait(g)
            e.ve.wait(g)
            e.se.wait(g)
            for dc in range(DC):
                e.g_ld.inc(g.dma_start(out=proj_bf[:, dc, :],
                                       in_=e.proj[dc * 128:(dc + 1) * 128, :]), 16)

        for which, w_h in (("q", e.wqT), ("k", e.wkT)):
            @block.gpsimd
            def _(g, w_h=w_h):
                e.te.wait(g)
                e.ve.wait(g)
                for dc in range(DC):
                    e.g_ld.inc(g.dma_start(out=wbuf[:, dc, :],
                                           in_=w_h[dc * 128:(dc + 1) * 128, :]), 16)

            with contextlib.ExitStack() as sq:
                ps_p = sq.enter_context(nc.psum_tensor("ps_p_" + which,
                                                       [128, 4, SL], F32))
                ntile = PC if which == "q" else 4
                tw = SL if which == "q" else P
                se_w = e.se.n
                ve_w = e.ve.n
                PL = DC + ntile

                def evac_m(lsl, ec, se_w=se_w, PL=PL):
                    return se_w + lsl * PL + ec + 1

                def elu_m(tix, se_w=se_w, PL=PL, ntile=ntile):
                    return se_w + (tix // ntile) * PL + DC + tix % ntile + 1

                for lsl in range(NSL):
                    cs = slice(lsl * SL, (lsl + 1) * SL)
                    te0 = e.te.n

                    @block.tensor
                    def _(t, cs=cs, lsl=lsl, which=which):
                        e.g_ld.wait(t)
                        if which == "q" and lsl == 0:
                            e.ve.wait(t)
                            e.se.wait(t)
                        for ec in range(DC):
                            if ec >= 2:
                                t.wait_ge(e.se.h, evac_m(lsl, ec - 2))
                            for dc in range(DC):
                                mm = t.matmul(ps_q[:, ec % 2, :],
                                              wbuf[:, dc, ec * 128:(ec + 1) * 128],
                                              xn_bf[:, dc, cs],
                                              start=(dc == 0), stop=(dc == DC - 1))
                            e.te.inc(mm)

                    @block.scalar
                    def _(s, te0=te0, lsl=lsl):
                        for ec in range(DC):
                            s.wait_ge(e.te.h, te0 + ec + 1)
                            e.se.inc(s.activation(qkbuf[lsl][:, ec, :],
                                                  ps_q[:, ec % 2, :],
                                                  AF.Identity))

                    te1 = e.te.n

                    @block.tensor
                    def _(t, lsl=lsl, which=which):
                        for ti in range(ntile):
                            tix = lsl * ntile + ti
                            t.wait_ge(e.se.h, evac_m(lsl, DC - 1))
                            if tix >= 4:
                                t.wait_ge(e.se.h, elu_m(tix - 4))
                            if which == "q":
                                for ec in range(DC):
                                    mm = t.matmul(
                                        ps_p[:, tix % 4, :],
                                        proj_bf[:, ec, ti * 128:(ti + 1) * 128],
                                        qkbuf[lsl][:, ec, :],
                                        start=(ec == 0), stop=(ec == DC - 1))
                            else:
                                for ec in range(DC):
                                    mm = t.matmul(
                                        ps_p[:, tix % 4, 0:P],
                                        qkbuf[lsl][:, ec, ti * 128:(ti + 1) * 128],
                                        proj_bf[:, ec, :],
                                        start=(ec == 0), stop=(ec == DC - 1))
                            e.te.inc(mm)

                    @block.scalar
                    def _(s, lsl=lsl, te1=te1, tw=tw):
                        for ti in range(ntile):
                            tix = lsl * ntile + ti
                            s.wait_ge(e.te.h, te1 + ti + 1)
                            if tix >= 2:
                                s.wait_ge(e.ve.h, ve_w + tix - 1)
                            s.activation(at[tix][:, 0:tw], ps_p[:, tix % 4, 0:tw],
                                         AF.Relu)
                            s.activation(bt[tix][:, 0:tw], ps_p[:, tix % 4, 0:tw],
                                         AF.Relu, scale=-1.0)
                            e.se.inc(s.activation(bt[tix][:, 0:tw],
                                                  bt[tix][:, 0:tw],
                                                  AF.Exp, scale=-1.0))

                    @block.vector
                    def _(v, lsl=lsl, cs=cs, which=which, tw=tw):
                        for ti in range(ntile):
                            tix = lsl * ntile + ti
                            v.wait_ge(e.se.h, elu_m(tix))
                            if which == "q":
                                dst = qp_bf[:, ti, cs]
                            else:
                                dst = kp_bf[:, lsl * 4 + ti, :]
                            e.ve.inc(v.tensor_tensor(dst, at[tix][:, 0:tw],
                                                     bt[tix][:, 0:tw], AO.add))

        @block.gpsimd
        def _(g):
            e.te.wait(g)
            for dc in range(DC):
                e.g_ld.inc(g.dma_start(out=wbuf[:, dc, :],
                                       in_=e.wvT[dc * 128:(dc + 1) * 128, :]), 16)

        with contextlib.ExitStack() as sv:
            vbuf = DBuf(nc, sv, "vbuf", [128, SL], BF16)
            ps_kv = sv.enter_context(nc.psum_tensor("ps_kv", [128, 4, SL], F32))
            te_v = e.te.n
            se_v = e.se.n

            @block.tensor
            def _(t):
                e.g_ld.wait(t)
                e.ve.wait(t)
                for j in range(LC * 2):
                    lc, dsl = j // 2, j % 2
                    if j >= 2:
                        t.wait_ge(e.se.h, se_v + j - 1)
                    for dc in range(DC):
                        mm = t.matmul(ps_q[:, j % 2, :],
                                      xn_bf[:, dc, lc * 128:(lc + 1) * 128],
                                      wbuf[:, dc, dsl * SL:(dsl + 1) * SL],
                                      start=(dc == 0), stop=(dc == DC - 1))
                    e.te.inc(mm)
                    if j >= 1:
                        pj = j - 1
                        plc, pdsl = pj // 2, pj % 2
                        t.wait_ge(e.se.h, se_v + pj + 1)
                        for pc in range(PC):
                            t.matmul(ps_kv[:, pc * 2 + pdsl, :],
                                     kp_bf[:, plc, pc * 128:(pc + 1) * 128],
                                     vbuf[pj % 2],
                                     start=(plc == 0), stop=(plc == LC - 1))
                pj = LC * 2 - 1
                plc, pdsl = pj // 2, pj % 2
                t.wait_ge(e.se.h, se_v + pj + 1)
                for pc in range(PC):
                    mm = t.matmul(ps_kv[:, pc * 2 + pdsl, :],
                                  kp_bf[:, plc, pc * 128:(pc + 1) * 128],
                                  vbuf[pj % 2],
                                  start=(plc == 0), stop=(plc == LC - 1))
                e.te.inc(mm)

            @block.scalar
            def _(s):
                for j in range(LC * 2):
                    s.wait_ge(e.te.h, te_v + j + 1)
                    e.se.inc(s.activation(vbuf[j % 2], ps_q[:, j % 2, :],
                                          AF.Identity))
                s.wait_ge(e.te.h, te_v + LC * 2 + 1)
                for pc in range(PC):
                    for dsl in range(2):
                        e.se.inc(s.activation(
                            kv_bf[:, pc, dsl * SL:(dsl + 1) * SL],
                            ps_kv[:, pc * 2 + dsl, :], AF.Identity))

        with contextlib.ExitStack() as sz:
            ps_z = sz.enter_context(nc.psum_tensor("ps_z", [1, SL], F32))

            for lsl in range(NSL):
                cs = slice(lsl * SL, (lsl + 1) * SL)

                @block.tensor
                def _(t, cs=cs):
                    e.ve.wait(t)
                    e.se.wait(t)
                    for pc in range(PC):
                        e.te.inc(t.matmul(ps_z[:], e.onecb[:], qp_bf[:, pc, cs], start=(pc == 0), stop=(pc == PC - 1)))

                @block.vector
                def _(v, cs=cs):
                    e.te.wait(v)
                    e.ve.inc(v.tensor_scalar(zsum[0:1, cs], ps_z[:], EPS_Z,
                                             None, AO.add))

            @block.scalar
            def _(s):
                e.ve.wait(s)
                s.activation(zrec[:], zsum[:], AF.Ln)
                e.se.inc(s.activation(zrec[:], zrec[:], AF.Exp, scale=-1.0))

            @block.vector
            def _(v):
                e.se.wait(v)
                e.ve.inc(v.tensor_copy(e.vrow[0:1, 0:8], zrec[0:1, 0:8]))

        with contextlib.ExitStack() as sn:
            zbs = sn.enter_context(nc.sbuf_tensor("zbs", [128, NSL, SL], F32))
            xrl = DBuf(nc, sn, "xrl", [128, SL], F32)
            xas = DBuf(nc, sn, "xas", [128, SL], F32)
            ps_zb = sn.enter_context(nc.psum_tensor("ps_zb", [128, 2, SL], F32))
            se_z = e.se.n
            te_z = e.te.n

            @block.tensor
            def _(t):
                e.ve.wait(t)
                for lsl in range(NSL):
                    cs = slice(lsl * SL, (lsl + 1) * SL)
                    if lsl >= 2:
                        t.wait_ge(e.se.h, se_z + lsl - 1)
                    e.te.inc(t.matmul(ps_zb[:, lsl % 2, :], e.onesr_s[:],
                                      zrec[0:1, cs], start=True, stop=True))

            @block.scalar
            def _(s):
                for lsl in range(NSL):
                    s.wait_ge(e.te.h, te_z + lsl + 1)
                    e.se.inc(s.activation(zbs[:, lsl, :], ps_zb[:, lsl % 2, :],
                                          AF.Identity))

            NJ = DC * NSL
            te_base = e.te.n
            ve_base = e.ve.n
            xrl_st = Stream(e, "xrlS")
            xas_st = Stream(e, "xasS")
            e.store_streams.append(xas_st)

            @block.sync
            def _(sy, ve_base=ve_base):
                e.te.wait(sy)
                e.ve.wait(sy)
                e.se.wait(sy)
                for j in range(NJ):
                    dc, lsl = j // NSL, j % NSL
                    if j >= 2:
                        sy.wait_ge(e.ve.h, ve_base + j - 1)
                    xrl_st.inc(sy.dma_start(
                        out=xrl[j % 2],
                        in_=e.xT[dc * 128:(dc + 1) * 128,
                                 lsl * SL:(lsl + 1) * SL]), j)

            @block.tensor
            def _(t, ve_base=ve_base):
                e.se.wait(t)
                for j in range(NJ):
                    dc, lsl = j // NSL, j % NSL
                    cs = slice(lsl * SL, (lsl + 1) * SL)
                    if j >= 2:
                        t.wait_ge(e.ve.h, ve_base + j - 1)
                    for pc in range(PC):
                        mm = t.matmul(ps_q[:, j % 2, :],
                                      kv_bf[:, pc, dc * 128:(dc + 1) * 128],
                                      qp_bf[:, pc, cs],
                                      start=(pc == 0), stop=(pc == PC - 1))
                    e.te.inc(mm)

            @block.vector
            def _(v, te_base=te_base):
                for j in range(NJ):
                    dc, lsl = j // NSL, j % NSL
                    v.wait_ge(e.te.h, te_base + j + 1)
                    xrl_st.wait_done(v, j)
                    if j >= 2:
                        xas_st.wait_done(v, j - 2)
                    v.tensor_tensor(xas[j % 2], ps_q[:, j % 2, :],
                                    zbs[:, lsl, :], AO.mult)
                    e.ve.inc(v.tensor_tensor(xas[j % 2], xas[j % 2], xrl[j % 2], AO.add))

            @block.gpsimd
            def _(g, ve_base=ve_base):
                for j in range(NJ):
                    dc, lsl = j // NSL, j % NSL
                    g.wait_ge(e.ve.h, ve_base + j + 1)
                    xas_st.inc(g.dma_start(
                        out=e.xatt_h[dc][:, lsl * SL:(lsl + 1) * SL],
                        in_=xas[j % 2]), j)


class WdqLoads:

    def __init__(self, e, wn_src, ncn, wnb):
        self.e = e
        self.wn_src = wn_src
        self.ncn = ncn
        self.wnb = wnb
        self.st = Stream(e, uniq("wnbS"), nslots=wnb.n)
        self.dqsem = Sem(e.nc, e.es, uniq("dqs"))
        self.trsem = Sem(e.nc, e.es, uniq("trs"))

    def emit(self, lo, hi):
        e, wnb, wn_src = self.e, self.wnb, self.wn_src
        st, dqsem = self.st, self.dqsem

        @e.block.sync
        def _(sy):
            for ic in range(lo, hi):
                if ic >= wnb.n:
                    sy.wait_ge(dqsem.h, ic - wnb.n + 1)
                st.inc(sy.dma_start(out=wnb[ic], in_=wn_src(ic)), ic)


def emit_wdq(e, ldstate, ncn, nb, st_all, rc_all, st_act, wnb, wnat, dqt,
             ps_tr, dest, tpc, prev_se):
    nc, block = e.nc, e.block
    wnb_st, dqsem, trsem = ldstate.st, ldstate.dqsem, ldstate.trsem
    vA = e.ve.n
    seA = e.se.n
    sc_marks = []
    evac_marks = []

    def ts_mark(c):
        return vA + c + 1

    @block.scalar
    def _(s):
        def evacs(ic):
            s.wait_ge(trsem.h, trsem.n + (ic + 1) * tpc)
            e.se.inc(s.activation(
                dest(ic), ps_tr[:, ic % 2, 0:1024]
                .rearrange("p (h i) -> p h i", h=tpc), AF.Identity))
            evac_marks.append(e.se.n)

        for ic in range(ncn):
            s.wait_ge(e.ve.h, ts_mark(ic))
            if ic >= 2:
                s.wait_ge(dqsem.h, dqsem.n + ic - 1)
            s.activation(rc_all[:, ic, :], st_all[:, ic, :], AF.Ln)
            s.activation(rc_all[:, ic, :], rc_all[:, ic, :], AF.Exp,
                         scale=-1.0)
            e.se.inc(s.copy(st_act[ic], st_all[:, ic, :]))
            sc_marks.append(e.se.n)
            if ic >= 2:
                evacs(ic - 2)
        for ic in range(max(ncn - 2, 0), ncn):
            evacs(ic)

    @block.vector
    def _(v):
        v.wait_ge(e.se.h, prev_se)

        def dq(ic):
            v.wait_ge(e.se.h, sc_marks[ic])
            if ic >= 2:
                v.wait_ge(trsem.h, trsem.n + (ic - 1) * tpc)
            v.tensor_tensor(dqt[0].rearrange("p (g b) -> p g b", b=32),
                            wnb[ic].rearrange("p (g b) -> p g b", b=32),
                            rc_all[:, ic, :].rearrange("p (g o) -> p g o", o=1)
                            .broadcast_to([128, nb, 32]), AO.mult)
            v.tensor_scalar(dqt[0], dqt[0], MAGIC, MAGIC, AO.add, AO.subtract)
            dqsem.inc(v.tensor_tensor(
                wnat[ic].rearrange("p (g b) -> p g b", b=32),
                dqt[0].rearrange("p (g b) -> p g b", b=32),
                st_act[ic].rearrange("p (g o) -> p g o", o=1)
                .broadcast_to([128, nb, 32]), AO.mult))

        for ic in range(ncn):
            wnb_st.wait_done(v, ic)
            v.tensor_reduce(st_all[:, ic, :],
                            wnb[ic].rearrange("p (g b) -> p g b", b=32),
                            AX.X, AO.max, apply_absolute_value=True)
            if ic >= 1:
                e.ve.inc(v.tensor_scalar(st_all[:, ic - 1, :],
                                         st_all[:, ic - 1, :],
                                         INV7, TINY, AO.mult, AO.max))
            if ic >= 2:
                dq(ic - 2)
        e.ve.inc(v.tensor_scalar(st_all[:, ncn - 1, :],
                                 st_all[:, ncn - 1, :],
                                 INV7, TINY, AO.mult, AO.max))
        for ic in range(max(ncn - 2, 0), ncn):
            dq(ic)

    @block.tensor
    def _(t):
        for ic in range(ncn):
            for h in range(tpc):
                if h == 0:
                    t.wait_ge(dqsem.h, dqsem.n - ncn + ic + 1)
                    if ic >= 2:
                        t.wait_ge(e.se.h, evac_marks[ic - 2])
                    else:
                        t.wait_ge(e.se.h, seA)
                trsem.inc(t.transpose(ps_tr[:, ic % 2, h * 128:(h + 1) * 128],
                                      wnat[ic][:, h * 128:(h + 1) * 128],
                                      e.ident_bf[:]))


def emit_ffn_f1(e, xq_bf, wdq, spill_h):
    nc, block = e.nc, e.block
    with contextlib.ExitStack() as sf:
        stg = DBuf(nc, sf, "stg_f1", [128, L], F16, nslots=3)
        ps_mm = sf.enter_context(nc.psum_tensor(uniq("ps_mm_f1"), [128, 4, SL], F32))

        te_base = e.te.n
        ve_base = e.ve.n
        se_base = e.se.n
        stg_st = Stream(e, "stgS")
        e.store_streams.append(stg_st)

        @block.tensor
        def _(t):
            e.ve.wait(t)
            e.se.wait(t)
            for j in range(IC * NSL):
                ic, lsl = j // NSL, j % NSL
                cs = slice(lsl * SL, (lsl + 1) * SL)
                if j >= 4:
                    t.wait_ge(e.se.h, se_base + j - 3)
                for hc in range(DC):
                    mm = t.matmul(ps_mm[:, j % 4, :],
                                  wdq[:, hc, ic * 128:(ic + 1) * 128],
                                  xq_bf[:, hc, cs],
                                  start=(hc == 0), stop=(hc == DC - 1))
                e.te.inc(mm)

        @block.scalar
        def _(s):
            for j in range(IC * NSL):
                ic, lsl = j // NSL, j % NSL
                cs = slice(lsl * SL, (lsl + 1) * SL)
                s.wait_ge(e.te.h, te_base + j + 1)
                if lsl == 0:
                    if ic >= 3:
                        stg_st.wait_done(s, ic - 3)
                        s.wait_ge(e.ve.h, ve_base + ic - 2)
                    elif ic == 0:
                        wait_stores(e, s)
                e.se.inc(s.activation(stg[ic][:, cs], ps_mm[:, j % 4, :],
                                      AF.Identity))

        @block.vector
        def _(v):
            for ic in range(IC):
                v.wait_ge(e.se.h, se_base + (ic + 1) * NSL)
                e.ve.inc(v.tensor_reduce(e.mxc[:, ic:ic + 1], stg[ic],
                                         AX.X, AO.max,
                                         apply_absolute_value=True))

        @block.sync
        def _(sy):
            for ic in range(IC):
                sy.wait_ge(e.se.h, se_base + (ic + 1) * NSL)
                stg_st.inc(sy.dma_start(out=spill_h[ic], in_=stg[ic]), ic)


def emit_sweep1(e):
    nc, block = e.nc, e.block
    with contextlib.ExitStack() as s:
        g16 = DBuf(nc, s, "g16", [128, L], F16)
        t1 = DBuf(nc, s, "sw_t1", [128, L], F32)
        t2 = DBuf(nc, s, "sw_t2", [128, L], F32)
        t3 = DBuf(nc, s, "sw_t3", [128, L], F32)
        gst = DBuf(nc, s, "gst", [128, L], F16)

        se0 = e.se.n
        ge0 = e.ge.n
        ve0 = e.ve.n
        g16_st = Stream(e, "g16S")
        gst_st = Stream(e, "gstS")
        pre1_marks = []
        tsd_marks = []
        red_marks = []

        def act_mark(ic):
            return se0 + ic + 1

        def tt_mark(p):
            return ge0 + p + 1

        def pre1_arith(ic):
            return ve0 + (ic + 1 if ic < 2 else 3 * ic - 2)

        @block.sync
        def _(sy):
            wait_stores(e, sy)
            for ic in range(IC):
                if ic >= 2:
                    sy.wait_ge(e.ve.h, pre1_arith(ic - 2))
                g16_st.inc(sy.dma_start(out=g16[ic], in_=e.gate_h[ic]), ic)

        e.store_streams.append(gst_st)

        @block.vector
        def _(v):
            wait_stores(e, v)

            for ic in range(IC):
                g16_st.wait_done(v, ic)
                if ic >= 2:
                    v.wait_ge(e.ge.h, tt_mark(ic - 2))
                e.ve.inc(v.tensor_scalar(t1[ic], g16[ic], e.c3t[:, 0:1],
                                         MAGIC, AO.mult, AO.add))
                pre1_marks.append(e.ve.n)
                if ic >= 1:
                    p = ic - 1
                    v.wait_ge(e.se.h, act_mark(p))
                    e.ve.inc(v.tensor_scalar(t1[p], t3[p], MAGIC65535, MAGIC,
                                             AO.min, AO.subtract))
                    tsd_marks.append(e.ve.n)
                if ic >= 2:
                    p = ic - 2
                    v.wait_ge(e.ge.h, tt_mark(p))
                    e.ve.inc(v.tensor_reduce(e.mxc[:, p:p + 1], gst[p],
                                             AX.X, AO.max,
                                             apply_absolute_value=True))
                    red_marks.append(e.ve.n)
            p = IC - 1
            v.wait_ge(e.se.h, act_mark(p))
            e.ve.inc(v.tensor_scalar(t1[p], t3[p], MAGIC65535, MAGIC,
                                     AO.min, AO.subtract))
            tsd_marks.append(e.ve.n)
            for p in (IC - 2, IC - 1):
                v.wait_ge(e.ge.h, tt_mark(p))
                e.ve.inc(v.tensor_reduce(e.mxc[:, p:p + 1], gst[p],
                                         AX.X, AO.max,
                                         apply_absolute_value=True))
                red_marks.append(e.ve.n)

        @block.scalar
        def _(s_):
            for ic in range(IC):
                s_.wait_ge(e.ve.h, pre1_marks[ic])
                if ic >= 2:
                    s_.wait_ge(e.ve.h, tsd_marks[ic - 2])
                    s_.wait_ge(e.ge.h, tt_mark(ic - 2))
                s_.activation(t2[ic], t1[ic], AF.Identity,
                              bias=e.nmagict[:, 0:1])
                s_.activation(t2[ic], t2[ic], AF.Identity,
                              scale=e.kkt[:, 0:1])
                s_.activation(t3[ic], t2[ic], AF.Sigmoid, scale=65536.0)
                e.se.inc(s_.activation(t3[ic], t3[ic], AF.Identity,
                                       bias=e.magict[:, 0:1], scale=65536.0))

        @block.gpsimd
        def _(g):
            for p in range(IC):
                g.wait_ge(e.ve.h, tsd_marks[p])
                if p >= 2:
                    g.wait_ge(e.ve.h, red_marks[p - 2])
                    gst_st.wait_done(g, p - 2)
                e.ge.inc(g.tensor_tensor(gst[p], t2[p], t1[p], AO.mult))
                gst_st.inc(g.dma_start(out=e.gs_h[p], in_=gst[p]), p)


def emit_sweep2(e):
    nc, block = e.nc, e.block
    with contextlib.ExitStack() as s:
        u16 = DBuf(nc, s, "u16", [128, L], F16)
        g16 = DBuf(nc, s, "gs16", [128, L], F16)
        t1 = DBuf(nc, s, "s2_t1", [128, L], F32)
        t2 = DBuf(nc, s, "s2_t2", [128, L], F32)
        ost = DBuf(nc, s, "ost", [128, L], F16)

        se0 = e.se.n
        ge0 = e.ge.n
        ve0 = e.ve.n
        u16_st = Stream(e, "u16S")
        gs16_st = Stream(e, "gs16S")
        ost_st = Stream(e, "ostS")
        gs_marks = []
        red_marks = []

        def act_mark(ic):
            return se0 + ic + 1

        def tt_mark(p):
            return ge0 + p + 1

        def gs_arith(ic):
            return ve0 + (ic + 1 if ic < 2 else 2 * ic - 1)

        @block.sync
        def _(sy):
            wait_stores(e, sy)
            for ic in range(IC):
                if ic >= 2:
                    sy.wait_ge(e.se.h, act_mark(ic - 2))
                    sy.wait_ge(e.ve.h, gs_arith(ic - 2))
                u16_st.inc(sy.dma_start(out=u16[ic], in_=e.up_h[ic]), ic)
                gs16_st.inc(sy.dma_start(out=g16[ic], in_=e.gs_h[ic]), ic)

        e.store_streams.append(ost_st)

        @block.vector
        def _(v):
            wait_stores(e, v)
            for ic in range(IC):
                gs16_st.wait_done(v, ic)
                if ic >= 2:
                    v.wait_ge(e.ge.h, tt_mark(ic - 2))
                v.tensor_scalar(t2[ic], g16[ic], e.c4t[:, 0:1], MAGIC,
                                AO.mult, AO.add)
                e.ve.inc(v.tensor_scalar(t2[ic], t2[ic], MAGIC, e.s4t[:, 0:1],
                                         AO.subtract, AO.mult))
                gs_marks.append(e.ve.n)
                if ic >= 2:
                    p = ic - 2
                    v.wait_ge(e.ge.h, tt_mark(p))
                    e.ve.inc(v.tensor_reduce(e.mxc[:, p:p + 1], ost[p],
                                             AX.X, AO.max,
                                             apply_absolute_value=True))
                    red_marks.append(e.ve.n)
            for p in (IC - 2, IC - 1):
                v.wait_ge(e.ge.h, tt_mark(p))
                e.ve.inc(v.tensor_reduce(e.mxc[:, p:p + 1], ost[p],
                                         AX.X, AO.max,
                                         apply_absolute_value=True))
                red_marks.append(e.ve.n)

        @block.scalar
        def _(s_):
            for ic in range(IC):
                u16_st.wait_done(s_, ic)
                if ic >= 2:
                    s_.wait_ge(e.ge.h, tt_mark(ic - 2))
                s_.activation(t1[ic], u16[ic], AF.Identity,
                              scale=e.c2t[:, 0:1], bias=e.magict[:, 0:1])
                s_.activation(t1[ic], t1[ic], AF.Identity,
                              bias=e.nmagict[:, 0:1])
                e.se.inc(s_.activation(t1[ic], t1[ic], AF.Identity,
                                       scale=e.s2t[:, 0:1]))

        @block.gpsimd
        def _(g):
            for p in range(IC):
                g.wait_ge(e.ve.h, gs_marks[p])
                g.wait_ge(e.se.h, act_mark(p))
                if p >= 2:
                    g.wait_ge(e.ve.h, red_marks[p - 2])
                    ost_st.wait_done(g, p - 2)
                e.ge.inc(g.tensor_tensor(ost[p], t1[p], t2[p], AO.mult))
                ost_st.inc(g.dma_start(out=e.gs_h[p], in_=ost[p]), p)


def emit_sweep3(e, wdq_dn):
    nc, block = e.nc, e.block
    with contextlib.ExitStack() as s:
        ob16 = DBuf(nc, s, "ob16", [128, SL], F16)
        obf = DBuf(nc, s, "obf", [128, IC, SL], BF16)
        yt = DBuf(nc, s, "yt_s3", [128, SL], F32)
        xrl2 = DBuf(nc, s, "xrl2", [128, SL], F32)
        ostg = DBuf(nc, s, "ostg", [128, SL], F32)
        ps_dn = s.enter_context(nc.psum_tensor("ps_dn", [128, 4, SL], F32))
        ob16_st = Stream(e, "ob16S")
        xrl2_st = Stream(e, "xrl2S")
        ostg_st = Stream(e, "ostgS")
        e.store_streams.append(ostg_st)

        se0 = e.se.n
        te0 = e.te.n
        conv_marks = {}
        ostg_marks = {}

        def act_mark(j):
            return se0 + j + 1

        def mm_mark(lsl, hc):
            return te0 + lsl * DC + hc + 1

        n = e.ve.n
        for lsl in range(NSL):
            for ic in range(IC):
                n += 1
                conv_marks[(lsl, ic)] = n
            if lsl >= 1:
                for hc in range(DC):
                    n += 1
                    ostg_marks[(lsl - 1, hc)] = n
        for hc in range(DC):
            n += 1
            ostg_marks[(NSL - 1, hc)] = n

        @block.scalar
        def _(s_):
            for lsl in range(NSL):
                for ic in range(IC):
                    j = lsl * IC + ic
                    ob16_st.wait_done(s_, j)
                    if j >= 2:
                        s_.wait_ge(e.ve.h, conv_marks[divmod(j - 2, IC)])
                    s_.activation(yt[j], ob16[j], AF.Identity,
                                  scale=e.c5t[:, 0:1], bias=e.magict[:, 0:1])
                    e.se.inc(s_.activation(yt[j], yt[j], AF.Identity,
                                           bias=e.nmagict[:, 0:1]))

        @block.vector
        def _(v):
            wait_stores(e, v)

            def convs(lsl):
                for ic in range(IC):
                    j = lsl * IC + ic
                    v.wait_ge(e.se.h, act_mark(j))
                    if lsl >= 2 and ic == 0:
                        v.wait_ge(e.te.h, mm_mark(lsl - 2, DC - 1))
                    e.ve.inc(v.tensor_scalar(obf[lsl][:, ic, :], yt[j],
                                             e.s5t[:, 0:1], None, AO.mult))

            def ostg_row(lsl):
                for hc in range(DC):
                    g = lsl * DC + hc
                    v.wait_ge(e.te.h, mm_mark(lsl, hc))
                    xrl2_st.wait_done(v, g)
                    if g >= 2:
                        ostg_st.wait_done(v, g - 2)
                    e.ve.inc(v.tensor_tensor(ostg[g], ps_dn[:, g % 4, :],
                                             xrl2[g], AO.add))

            for lsl in range(NSL):
                convs(lsl)
                if lsl >= 1:
                    ostg_row(lsl - 1)
            ostg_row(NSL - 1)

        @block.sync
        def _(sy):
            wait_stores(e, sy)
            for lsl in range(NSL):
                for ic in range(IC):
                    j = lsl * IC + ic
                    if j >= 2:
                        sy.wait_ge(e.se.h, act_mark(j - 2))
                    ob16_st.inc(sy.dma_start(
                        out=ob16[j],
                        in_=e.gs_h[ic][:, lsl * SL:(lsl + 1) * SL]), j)

        @block.tensor
        def _(t):
            t.wait_ge(e.se.h, se0)
            for lsl in range(NSL):
                cs = slice(lsl * SL, (lsl + 1) * SL)
                for hc in range(DC):
                    g = lsl * DC + hc
                    if hc == 0:
                        t.wait_ge(e.ve.h, conv_marks[(lsl, IC - 1)])
                    if g >= 4:
                        t.wait_ge(e.ve.h, ostg_marks[divmod(g - 4, DC)])
                    for ic in range(IC):
                        mm = t.matmul(ps_dn[:, g % 4, :],
                                      wdq_dn[:, ic, hc * 128:(hc + 1) * 128],
                                      obf[lsl][:, ic, :],
                                      start=(ic == 0), stop=(ic == IC - 1))
                    e.te.inc(mm)

        NJ = NSL * DC

        def store_out(g_, g):
            lsl, hc = divmod(g, DC)
            g_.wait_ge(e.ve.h, ostg_marks[(lsl, hc)])
            ostg_st.inc(g_.dma_start(
                out=e.out[hc * 128:(hc + 1) * 128,
                          lsl * SL:(lsl + 1) * SL],
                in_=ostg[g]), g)

        @block.gpsimd
        def _(g_):
            wait_stores(e, g_)
            for g in range(NJ):
                if g >= 2:
                    store_out(g_, g - 2)
                lsl, hc = divmod(g, DC)
                xrl2_st.inc(g_.dma_start(
                    out=xrl2[g],
                    in_=e.xatt_h[hc][:, lsl * SL:(lsl + 1) * SL]), g)
            store_out(g_, NJ - 2)
            store_out(g_, NJ - 1)



def emit_program(e):
    nc, block = e.nc, e.block

    @block.sync
    def _(sy):
        e.s_ld.inc(sy.dma_start(out=e.onesr_s[:], in_=e.onesr[:]), 16)

    @block.tensor
    def _(t):
        e.s_ld.wait(t)
        e.te.inc(t.matmul(e.ps_misc[:, 0:128], e.onesr_s[:], e.onesr_s[:],
                          start=True, stop=True))

    @block.gpsimd
    def _(g):
        e.g_ld.inc(g.dma_start(out=e.ident_bf[:], in_=e.ident[:]), 16)
        e.ge.inc(g.memset(e.onecf[:], 1.0))
        e.ge.inc(g.memset(e.onecb[:], 1.0))
        e.ge.inc(g.memset(e.mloc[:], 0.0))
        e.ge.inc(g.memset(e.magict[:], MAGIC))
        e.ge.inc(g.memset(e.nmagict[:], -MAGIC))

    def set_kk(s):
        return s.activation(e.kkt[:], e.s3t[:], AF.Identity, scale=K216)

    with contextlib.ExitStack() as sc:
        wdq = sc.enter_context(nc.sbuf_tensor("wdq", [128, DC * I], BF16))
        wdq_ud = wdq[:].rearrange("p (h i) -> p h i", h=DC)
        wdq_dn = wdq[:].rearrange("p (i h) -> p i h", i=IC)
        wnb = DBuf(nc, sc, "wnb", [128, D], F32, nslots=4)
        st_all = sc.enter_context(nc.sbuf_tensor("st_all", [128, IC, 32], F32))
        rc_all = sc.enter_context(nc.sbuf_tensor("rc_all", [128, IC, 32], F32))
        st_act = DBuf(nc, sc, "st_act", [128, 32], F32)
        wnat = DBuf(nc, sc, "wnat", [128, D], BF16)
        dqt = DBuf(nc, sc, "dqt", [128, D], F32, nslots=1)
        dest_ud = lambda ic: wdq_ud[:, 0:DC, ic * 128:(ic + 1) * 128]

        xn_bf = sc.enter_context(nc.sbuf_tensor("xn_bf", [128, DC, L], BF16))
        emit_ln(e, xn_bf)
        ld_gate = WdqLoads(
            e, lambda ic: e.wgate_n[ic * 128:(ic + 1) * 128, :], IC, wnb)
        ld_gate.emit(0, wnb.n)
        emit_attention(e, xn_bf)
        ld_gate.emit(wnb.n, IC)

        stage_maxcols_to_cc(e, e.mxc[:, 0:32], 32, 0)
        collective_max(e, 1024, 0)

        ps_tr = sc.enter_context(nc.psum_tensor("ps_tr", [128, 2, 1024],
                                                BF16))
        prev_se = e.se.n
        emit_wdq(e, ld_gate, IC, 32, st_all, rc_all, st_act, wnb, wnat,
                 dqt, ps_tr, dest_ud, DC, prev_se)

        bcast_scales(e, [(0, 1024, 0, e.s1t, e.c1t, None)])
        xq_bf = xn_bf
        emit_xq(e, xn_bf, xq_bf)

        emit_ffn_f1(e, xq_bf, wdq_ud, e.gate_h)
        stage_maxcols_to_cc(e, e.mxc[:, 0:IC], IC, 1)
        collective_max(e, 1024, 1)
        ld_up = WdqLoads(
            e, lambda ic: e.wup_n[ic * 128:(ic + 1) * 128, :], IC, wnb)
        ld_up.emit(0, IC)
        prev_se = e.se.n
        emit_wdq(e, ld_up, IC, 32, st_all, rc_all, st_act, wnb, wnat,
                 dqt, ps_tr, dest_ud, DC, prev_se)
        bcast_scales(e, [(1, 1024, 1, e.s3t, e.c3t, set_kk)])

        emit_ffn_f1(e, xq_bf, wdq_ud, e.up_h)
        stage_maxcols_to_cc(e, e.mxc[:, 0:IC], IC, 0)
        collective_max(e, 1024, 0)

        emit_sweep1(e)
        ld_dn = WdqLoads(
            e, lambda ic: e.wdown_n[(ic // 3) * 128:(ic // 3 + 1) * 128,
                                    (ic % 3) * 1024:(ic % 3 + 1) * 1024],
            IC, wnb)
        ld_dn.emit(0, IC)
        stage_maxcols_to_cc(e, e.mxc[:, 0:IC], IC, 1)
        collective_max(e, 1024, 1)
        bcast_scales(e, [(0, 1024, 0, e.s2t, e.c2t, None)])

        prev_se = e.se.n
        emit_wdq(e, ld_dn, IC, 32, st_all, rc_all, st_act, wnb, wnat,
                 dqt, ps_tr,
                 lambda ic: wdq_dn[:, (ic % 3) * 8:(ic % 3) * 8 + 8,
                                   (ic // 3) * 128:(ic // 3 + 1) * 128],
                 DC, prev_se)
        bcast_scales(e, [(1, 1024, 1, e.s4t, e.c4t, None)])

        emit_sweep2(e)
        stage_maxcols_to_cc(e, e.mxc[:, 0:IC], IC, 0)
        collective_max(e, 1024, 0)
        bcast_scales(e, [(0, 1024, 0, e.s5t, e.c5t, None)])

        emit_sweep3(e, wdq_dn)

    @block.sync
    def _(sy):
        wait_stores(e, sy)



def kernel(L0_attn_x, L0_attn_Wq, L0_attn_Wk, L0_attn_Wv, L0_attn_proj,
           L0_ffn_w_gate, L0_ffn_w_up, L0_ffn_w_down):
    X = np.ascontiguousarray(np.asarray(L0_attn_x, dtype=np.float32))
    Wq = np.asarray(L0_attn_Wq, dtype=np.float32)
    Wk = np.asarray(L0_attn_Wk, dtype=np.float32)
    Wv = np.asarray(L0_attn_Wv, dtype=np.float32)
    proj = np.ascontiguousarray(np.asarray(L0_attn_proj, dtype=np.float32))
    w_gate = np.ascontiguousarray(np.asarray(L0_ffn_w_gate, dtype=np.float32))
    w_up = np.ascontiguousarray(np.asarray(L0_ffn_w_up, dtype=np.float32))
    w_down = np.ascontiguousarray(np.asarray(L0_ffn_w_down, dtype=np.float32))

    wqT = np.ascontiguousarray(Wq.T)
    wkT = np.ascontiguousarray(Wk.T)
    wvT = np.ascontiguousarray(Wv.T)

    ident = np.eye(128, dtype=np.float32)
    onesr = np.ones((1, 128), dtype=np.float32)

    nc = build_nc()
    in_maps = []
    for b in range(B):
        in_maps.append({
            "xT": np.ascontiguousarray(X[b].T),
            "wqT": wqT, "wkT": wkT, "wvT": wvT, "proj": proj,
            "wup_n": w_up, "wgate_n": w_gate, "wdown_n": w_down,
            "ident": ident, "onesr": onesr,
        })
    res = run_bass_kernel_spmd(nc, in_maps, core_ids=list(range(B)))
    out = np.stack([np.asarray(res.results[b]["out"]).T for b in range(B)], axis=0)
    return np.ascontiguousarray(out, dtype=np.float32)
